# version 13
# baseline (speedup 1.0000x reference)
import numpy as np
import ml_dtypes
import concourse.bass as bass
import concourse.mybir as mybir
from concourse.bass_utils import run_bass_kernel_spmd
from contextlib import ExitStack
from collections import deque

F32 = mybir.dt.float32
BF16 = mybir.dt.bfloat16
AF = mybir.ActivationFunctionType
ALU = mybir.AluOpType
AX = mybir.AxisListType

NCORE = 8
S = 1024
NT = S // 128
D = 1024
DB = D // 128
LATD = 256
LB = LATD // 128
FF = 2048
E = 8
H = 16
HD = 64
EPS = 1e-5

ENGINES = ("sync", "gpsimd", "tensor", "vector", "scalar")


class Plan:
    def __init__(self):
        self.ops = {e: [] for e in ENGINES}
        self.cnt = {}

    def sem(self, name):
        self.cnt[name] = 0

    def op(self, eng, fn, waits=(), inc=None, amt=1):
        ws = tuple((s, v) for (s, v) in waits if v > 0)
        self.ops[eng].append((ws, fn, inc, amt))
        if inc is not None:
            self.cnt[inc] += amt
            return (inc, self.cnt[inc])
        return None

    def tok(self, name):
        return (name, self.cnt[name])


def replay(plan, name, eng, sems):
    waited = {}
    for ws, fn, inc, amt in plan.ops[name]:
        for s, v in ws:
            if waited.get(s, 0) >= v:
                continue
            eng.wait_ge(sems[s], v)
            waited[s] = v
        ins = fn(eng)
        if inc is not None:
            ins.then_inc(sems[inc], amt)


def build_program(nsub):
    ntok = nsub * S
    nc = bass.Bass("TRN2", target_bir_lowering=False)
    x_d = nc.dram_tensor("x0", [ntok, D], F32, kind="ExternalInput").ap()
    wq_d = nc.dram_tensor("wq", [D, D], F32, kind="ExternalInput").ap()
    bq_d = nc.dram_tensor("bq", [D], F32, kind="ExternalInput").ap()
    wdn_d = nc.dram_tensor("wdn", [D, LATD], F32, kind="ExternalInput").ap()
    bdn_d = nc.dram_tensor("bdn", [LATD], F32, kind="ExternalInput").ap()
    wup_d = nc.dram_tensor("wup", [LATD, 2 * D], F32, kind="ExternalInput").ap()
    bup_d = nc.dram_tensor("bup", [2 * D], F32, kind="ExternalInput").ap()
    wo_d = nc.dram_tensor("wo", [D, D], F32, kind="ExternalInput").ap()
    bo_d = nc.dram_tensor("bo", [D], F32, kind="ExternalInput").ap()
    rt_d = nc.dram_tensor("rt", [D, E], F32, kind="ExternalInput").ap()
    wg_d = nc.dram_tensor("wg", [E, D, FF], F32, kind="ExternalInput").ap()
    wu_d = nc.dram_tensor("wu", [E, D, FF], F32, kind="ExternalInput").ap()
    wd_d = nc.dram_tensor("wd", [E, FF, D], F32, kind="ExternalInput").ap()
    id_d = nc.dram_tensor("idm", [128, 128], F32, kind="ExternalInput").ap()
    out_d = nc.dram_tensor("out", [ntok, D], F32, kind="ExternalOutput").ap()

    p = Plan()
    SEMS = ("di", "dj", "dk", "dr", "dc0", "dc1", "dg0", "dg1", "du0", "du1",
            "dd0", "dd1", "dd2", "dwo", "t", "v", "s", "so")
    for sname in SEMS:
        p.sem(sname)

    with ExitStack() as ctx:
        en = ctx.enter_context
        wq_s = en(nc.sbuf_tensor("wq_s", [128, DB, D], BF16))
        wdn_s = en(nc.sbuf_tensor("wdn_s", [128, DB, LATD], BF16))
        wup_s = en(nc.sbuf_tensor("wup_s", [128, LB, 2 * D], BF16))
        rt_s = en(nc.sbuf_tensor("rt_s", [128, DB, E], BF16))
        bq_r = en(nc.sbuf_tensor("bq_r", [1, D], BF16))
        bup_r = en(nc.sbuf_tensor("bup_r", [1, 2 * D], BF16))
        bo_r = en(nc.sbuf_tensor("bo_r", [1, D], BF16))
        bdn_c = en(nc.sbuf_tensor("bdn_c", [128, LB], F32))
        idb = en(nc.sbuf_tensor("idb", [128, 128], BF16))
        idf8 = en(nc.sbuf_tensor("idf8", [8, 8], F32))
        ones_r = en(nc.sbuf_tensor("ones_r", [1, 128], BF16))
        res3 = en(nc.sbuf_tensor("res3", [128, NT, D], F32))
        bufP = [en(nc.sbuf_tensor(f"bufP{i}", [128, DB, S], BF16)) for i in range(2)]
        yfm = en(nc.sbuf_tensor("yfm", [128, DB, S], BF16))
        ofm = en(nc.sbuf_tensor("ofm", [128, DB, S], BF16))
        latfm = en(nc.sbuf_tensor("latfm", [128, LB, S], BF16))
        wdq = en(nc.sbuf_tensor("wdq", [128, 3, 4, D], BF16))
        wgq = en(nc.sbuf_tensor("wgq", [128, 2, DB, 256], BF16))
        wuq = en(nc.sbuf_tensor("wuq", [128, 2, DB, 256], BF16))
        qtm = en(nc.sbuf_tensor("qtm", [128, 2, D], BF16))
        kvtm = en(nc.sbuf_tensor("kvtm", [128, 2, 2 * D], BF16))
        tmp3 = en(nc.sbuf_tensor("tmp3", [128, 3, D], BF16))
        xbf = en(nc.sbuf_tensor("xbf", [128, 2, D], BF16))
        otm = en(nc.sbuf_tensor("otm", [128, D], BF16))
        fg = en(nc.sbuf_tensor("fg", [128, D], BF16))
        att = en(nc.sbuf_tensor("att", [128, H * H], F32))
        attp = en(nc.sbuf_tensor("attp", [128, H * H], BF16))
        amax = en(nc.sbuf_tensor("amax", [128, H], F32))
        asum = en(nc.sbuf_tensor("asum", [128, H], F32))
        arec = en(nc.sbuf_tensor("arec", [128, H], F32))
        ltm = en(nc.sbuf_tensor("ltm", [128, NT * E], F32))
        probs = en(nc.sbuf_tensor("probs", [128, NT * E], F32))
        m1b = en(nc.sbuf_tensor("m1b", [128, NT * E], F32))
        m2b = en(nc.sbuf_tensor("m2b", [128, NT * E], F32))
        wfull = en(nc.sbuf_tensor("wfull", [128, NT * E], F32))
        p1b = en(nc.sbuf_tensor("p1b", [128, NT], F32))
        p2b = en(nc.sbuf_tensor("p2b", [128, NT], F32))
        d21 = en(nc.sbuf_tensor("d21", [128, NT], F32))
        texp = en(nc.sbuf_tensor("texp", [128, NT], F32))
        n1b = en(nc.sbuf_tensor("n1b", [128, NT], F32))
        n2b = en(nc.sbuf_tensor("n2b", [128, NT], F32))
        ssum = en(nc.sbuf_tensor("ssum", [128, NT], F32))
        ssq = en(nc.sbuf_tensor("ssq", [128, NT], F32))
        mall = en(nc.sbuf_tensor("mall", [128, NT], F32))
        msq = en(nc.sbuf_tensor("msq", [128, NT], F32))
        varb = en(nc.sbuf_tensor("varb", [128, NT], F32))
        rstd = en(nc.sbuf_tensor("rstd", [128, NT], F32))
        P = [en(nc.psum_tensor(f"P{i}", [128, D], F32)) for i in range(3)]
        pT = en(nc.psum_tensor("pT", [128, D], BF16))
        pT2 = en(nc.psum_tensor("pT2", [128, D], BF16))

        sems = {s_: en(nc.semaphore(s_)) for s_ in SEMS}

        woA = wdq[:, 0, :, :]
        woB = wdq[:, 1, :, :]

        st = dict(
            P_free=[("s", 0)] * 3, rotP=0,
            pT_free=("s", 0), pT2_free=("s", 0),
            xbf_free=[("t", 0), ("t", 0)],
            qtm_free=[("v", 0), ("v", 0)], kv_free=[("v", 0), ("v", 0)],
            otm_free=("t", 0),
            fg_free=[("v", 0), ("v", 0)],
            yfm_half=[("s", 0), ("s", 0)], tok_w_half=[("v", 0), ("v", 0)],
            wg_free=[("t", 0), ("t", 0)], wu_free=[("t", 0), ("t", 0)],
            wd_free=[("t", 0), ("t", 0), ("t", 0)],
            bufP_free=[("t", 0), ("t", 0)],
            yfm_free=("t", 0), ofm_free=("t", 0), lat_free=("t", 0),
            res_free=("so", 0), wo_loaded=("dwo", 0), wo_free=("t", 0),
            prev_exp=("s", 0), tmp_free=("v", 0), sq_last=("s", 0), sum_last=("v", 0),
        )

        def pnext():
            i = st["rotP"] % 3
            st["rotP"] += 1
            return i

        p.op("sync", lambda e: e.dma_start(out=res3[0:LB, 0, 0:128], in_=bdn_d.rearrange("(c p) -> c p", p=128)), inc="dj", amt=16)
        p.op("sync", lambda e: e.dma_start(out=idf8[:], in_=id_d[0:8, 0:8]), inc="dj", amt=16)
        init_dj = p.tok("dj")
        p.op("gpsimd", lambda e: e.dma_start(out=idb[:], in_=id_d), inc="dk", amt=16)
        init_dk = p.tok("dk")
        for dst, src_ in (
            (wq_s[:], wq_d.rearrange("(b p) n -> p b n", p=128)),
            (wdn_s[:], wdn_d.rearrange("(b p) n -> p b n", p=128)),
            (wup_s[:], wup_d.rearrange("(b p) n -> p b n", p=128)),
            (rt_s[:], rt_d.rearrange("(b p) n -> p b n", p=128)),
            (bq_r[:], bq_d[None, :]),
            (bup_r[:], bup_d[None, :]),
            (bo_r[:], bo_d[None, :]),
        ):
            p.op("gpsimd", lambda e, d=dst, s_=src_: e.dma_start(out=d, in_=s_), inc="di", amt=16)
        init_di = p.tok("di")
        p.op("vector", lambda e: e.memset(ones_r[:], 1.0), inc="v")
        tok_ones = p.tok("v")
        p.op("tensor", lambda e: e.transpose(P[0][:, 0:LB], res3[0:LB, 0, 0:128], idf8[0:LB, 0:LB]),
             waits=[init_dj], inc="t")
        p.op("scalar", lambda e: e.copy(bdn_c[:], P[0][:, 0:LB]), waits=[p.tok("t")], inc="s")
        tok_bdnc = p.tok("s")
        st["P_free"][0] = tok_bdnc

        def gen_xcast(s_i, tt):
            par = s_i % 2
            b = tt % 2
            r0 = s_i * S + tt * 128
            p.op("gpsimd", lambda e, r=r0, b=b: e.dma_start(
                out=xbf[:, b, :], in_=x_d[r:r + 128, :]),
                waits=[st["xbf_free"][b]], inc=f"dc{b}", amt=16)
            tok_cast = p.tok(f"dc{b}")
            yield

            def xtr(e, b=b):
                for blk in range(DB):
                    ins = e.transpose(pT[:, blk * 128:(blk + 1) * 128],
                                      xbf[:, b, blk * 128:(blk + 1) * 128], idb[:])
                return ins
            p.op("tensor", xtr, waits=[tok_cast, st["pT_free"], init_dk,
                                       st["bufP_free"][par]], inc="t")
            tok_xtr = p.tok("t")
            st["xbf_free"][b] = tok_xtr
            yield
            p.op("scalar", lambda e, tt=tt, par=par: e.copy(
                bufP[par][:, :, tt * 128:(tt + 1) * 128],
                pT[:].rearrange("p (b c) -> p b c", c=128)),
                waits=[tok_xtr], inc="s")
            st["pT_free"] = p.tok("s")
            st[f"xfm_ready_{tt}"] = st["pT_free"]
            yield

        def gen_lat(s_i):
            par = s_i % 2
            for lb in range(LB):
                pi = pnext()
                def latmm(e, lb=lb, pi=pi, par=par):
                    for nb in range(2):
                        for db in range(DB):
                            ins = e.matmul(P[pi][:, nb * 512:(nb + 1) * 512],
                                           wdn_s[:, db, lb * 128:(lb + 1) * 128],
                                           bufP[par][:, db, nb * 512:(nb + 1) * 512],
                                           start=(db == 0), stop=(db == DB - 1))
                    return ins
                p.op("tensor", latmm,
                     waits=[st["xfm_ready_7"], st["P_free"][pi], st["lat_free"], tok_bdnc,
                            init_di],
                     inc="t")
                tok_lat = p.tok("t")
                yield
                p.op("scalar", lambda e, lb=lb, pi=pi: e.activation(
                    latfm[:, lb, :], P[pi][:], AF.Identity, bias=bdn_c[:, lb:lb + 1]),
                    waits=[tok_lat], inc="s")
                st["P_free"][pi] = p.tok("s")
                st["lat_ready"] = p.tok("s")
                yield

        def gen_qkv(s_i, tt):
            par = s_i % 2
            pi = pnext()
            def qmm(e, tt=tt, pi=pi, par=par):
                for nb in range(2):
                    for db in range(DB):
                        e.matmul(P[pi][:, nb * 512:(nb + 1) * 512],
                                 bufP[par][:, db, tt * 128:(tt + 1) * 128],
                                 wq_s[:, db, nb * 512:(nb + 1) * 512],
                                 start=(db == 0), stop=False)
                    ins = e.matmul(P[pi][:, nb * 512:(nb + 1) * 512],
                                   ones_r[0:1, 0:128],
                                   bq_r[0:1, nb * 512:(nb + 1) * 512],
                                   start=False, stop=True)
                return ins
            p.op("tensor", qmm,
                 waits=[st[f"xfm_ready_{tt}"], st["P_free"][pi], init_di, tok_ones],
                 inc="t")
            tok_q = p.tok("t")
            yield
            b = tt % 2
            p.op("scalar", lambda e, pi=pi, b=b: e.activation(
                qtm[:, b, :], P[pi][:], AF.Identity, scale=np.float32(0.125)),
                waits=[tok_q, st["qtm_free"][b]], inc="s")
            st[f"tok_qe_{tt}"] = p.tok("s")
            st["P_free"][pi] = st[f"tok_qe_{tt}"]
            yield
            for half in range(2):
                pi = pnext()
                def kvmm(e, tt=tt, half=half, pi=pi):
                    for nb in range(2):
                        c0 = half * D + nb * 512
                        for lb in range(LB):
                            e.matmul(P[pi][:, nb * 512:(nb + 1) * 512],
                                     latfm[:, lb, tt * 128:(tt + 1) * 128],
                                     wup_s[:, lb, c0:c0 + 512],
                                     start=(lb == 0), stop=False)
                        ins = e.matmul(P[pi][:, nb * 512:(nb + 1) * 512],
                                       ones_r[0:1, 0:128],
                                       bup_r[0:1, c0:c0 + 512],
                                       start=False, stop=True)
                    return ins
                p.op("tensor", kvmm, waits=[st["lat_ready"], st["P_free"][pi]], inc="t")
                tok_kv = p.tok("t")
                yield
                p.op("scalar", lambda e, half=half, pi=pi, b=b: e.copy(
                    kvtm[:, b, half * D:(half + 1) * D], P[pi][:]),
                    waits=[tok_kv, st["kv_free"][b]], inc="s")
                st["P_free"][pi] = p.tok("s")
                yield
            st[f"tok_kve_{tt}"] = p.tok("s")

        def gen_attvec(s_i, tt):
            b = tt % 2
            kview = kvtm[:, b, 0:D].rearrange("p (g d) -> p g d", d=HD)
            vview = kvtm[:, b, D:2 * D].rearrange("p (d g) -> p d g", g=H)
            tok_qe = st[f"tok_qe_{tt}"]
            tok_kve = st[f"tok_kve_{tt}"]
            tok_m = [None] * H
            tok_r = [None] * H

            def qkm(h):
                def f(e, h=h, b=b, kview=kview):
                    return e.tensor_tensor(
                        tmp3[:, h % 3, :].rearrange("p (g d) -> p g d", d=HD),
                        qtm[:, b, h * HD:(h + 1) * HD].unsqueeze(1).broadcast_to([128, H, HD]),
                        kview, ALU.mult)
                return f

            def qkr(h):
                def f(e, h=h):
                    return e.reduce_sum(
                        att[:, h * H:(h + 1) * H],
                        tmp3[:, h % 3, :].rearrange("p (g d) -> p g d", d=HD), axis=AX.X)
                return f

            def emit_m(h):
                w = [tok_qe, tok_kve, st["prev_exp"], st["tmp_free"]] if h == 0 else (
                    [tok_r[h - 3]] if h >= 3 else [])
                p.op("vector", qkm(h), waits=w, inc="v")
                tok_m[h] = p.tok("v")

            def emit_r(h):
                p.op("vector", qkr(h), waits=[tok_m[h]], inc="v")
                tok_r[h] = p.tok("v")

            emit_m(0)
            yield
            emit_m(1)
            yield
            for h in range(2, H):
                emit_m(h)
                emit_r(h - 2)
                yield
            emit_r(H - 2)
            emit_r(H - 1)
            yield
            p.op("vector", lambda e: e.reduce_max(
                amax[:], att[:].rearrange("p (h g) -> p h g", g=H), axis=AX.X),
                waits=[tok_r[H - 1]], inc="v")
            tok_rmax = p.tok("v")
            p.op("vector", lambda e: e.tensor_tensor(
                att[:].rearrange("p (h g) -> p h g", g=H),
                att[:].rearrange("p (h g) -> p h g", g=H),
                amax[:].unsqueeze(2).broadcast_to([128, H, H]), ALU.subtract),
                waits=[tok_rmax], inc="v")
            tok_attm = p.tok("v")
            st["qtm_free"][b] = tok_attm
            yield
            p.op("scalar", lambda e: e.activation(attp[:], att[:], AF.Exp),
                 waits=[tok_attm], inc="s")
            tok_exp = p.tok("s")
            st["prev_exp"] = tok_exp
            yield
            p.op("vector", lambda e: e.reduce_sum(
                asum[:], attp[:].rearrange("p (h g) -> p h g", g=H), axis=AX.X),
                waits=[tok_exp], inc="v")
            tok_esum = p.tok("v")
            p.op("vector", lambda e: e.reciprocal(arec[:], asum[:]),
                 waits=[tok_esum], inc="v")
            tok_arec = p.tok("v")
            p.op("vector", lambda e: e.tensor_tensor(
                attp[:].rearrange("p (h g) -> p h g", g=H),
                attp[:].rearrange("p (h g) -> p h g", g=H),
                arec[:].unsqueeze(2).broadcast_to([128, H, H]), ALU.mult),
                waits=[tok_arec], inc="v")
            tok_probs = p.tok("v")
            yield

            tok_am = [None] * H
            tok_ar = [None] * H

            def avm(h):
                def f(e, h=h, vview=vview):
                    return e.tensor_tensor(
                        tmp3[:, h % 3, :].rearrange("p (d g) -> p d g", g=H),
                        attp[:, h * H:(h + 1) * H].unsqueeze(1).broadcast_to([128, HD, H]),
                        vview, ALU.mult)
                return f

            def avr(h):
                def f(e, h=h):
                    with e.bass.allow_low_precision(reason="bf16 matmul operand"):
                        return e.reduce_sum(
                            otm[:, h * HD:(h + 1) * HD],
                            tmp3[:, h % 3, :].rearrange("p (d g) -> p d g", g=H), axis=AX.X)
                return f

            def emit_am(h):
                w = [tok_probs, st["otm_free"]] if h == 0 else (
                    [tok_ar[h - 3]] if h >= 3 else [])
                p.op("vector", avm(h), waits=w, inc="v")
                tok_am[h] = p.tok("v")

            def emit_ar(h):
                p.op("vector", avr(h), waits=[tok_am[h]], inc="v")
                tok_ar[h] = p.tok("v")

            emit_am(0)
            yield
            emit_am(1)
            yield
            for h in range(2, H):
                emit_am(h)
                emit_ar(h - 2)
                yield
            emit_ar(H - 2)
            emit_ar(H - 1)
            tok_o = tok_ar[H - 1]
            st["kv_free"][b] = tok_o
            st["tmp_free"] = tok_o
            yield
            def otr(e):
                for blk in range(DB):
                    ins = e.transpose(pT2[:, blk * 128:(blk + 1) * 128],
                                      otm[:, blk * 128:(blk + 1) * 128], idb[:])
                return ins
            p.op("tensor", otr, waits=[tok_o, st["pT2_free"],
                                       st["ofm_free"] if tt == 0 else ("t", 0)],
                 inc="t")
            tok_otr = p.tok("t")
            st["otm_free"] = tok_otr
            yield
            p.op("scalar", lambda e, tt=tt: e.copy(
                ofm[:, :, tt * 128:(tt + 1) * 128],
                pT2[:].rearrange("p (b c) -> p b c", c=128)),
                waits=[tok_otr], inc="s")
            st["pT2_free"] = p.tok("s")
            st[f"ofm_ready_{tt}"] = st["pT2_free"]
            yield

        def pump(gens, n):
            k = 0
            while gens and k < n:
                try:
                    next(gens[0])
                    k += 1
                except StopIteration:
                    gens.popleft()

        def flush(gens):
            while gens:
                try:
                    next(gens[0])
                except StopIteration:
                    gens.popleft()

        def emit_ln_stats(tt, wait_tok):
            def sq(e, tt=tt):
                with e.bass.allow_low_precision(reason="scratch write; accum is fp32"):
                    return e.activation(xbf[:, 0, :], res3[:, tt, :], AF.Square,
                                        accum_out=ssq[:, tt:tt + 1])
            p.op("scalar", sq, waits=[wait_tok, st["sq_last"], st["xbf_free"][0]], inc="s")
            st["sq_last"] = p.tok("s")
            p.op("vector", lambda e, tt=tt: e.reduce_sum(
                ssum[:, tt:tt + 1], res3[:, tt, :], axis=AX.X),
                waits=[wait_tok], inc="v")
            st["sum_last"] = p.tok("v")

        def emit_ln_finish(lo, hi):
            sq_last = st["sq_last"]
            sl = slice(lo, hi)
            p.op("vector", lambda e: e.tensor_scalar_mul(mall[:, sl], ssum[:, sl], 1.0 / D),
                 waits=[st["sum_last"]], inc="v")
            tok_mall = p.tok("v")
            p.op("vector", lambda e: e.tensor_tensor(msq[:, sl], mall[:, sl], mall[:, sl], ALU.mult),
                 waits=[tok_mall], inc="v")
            tok_msq = p.tok("v")
            p.op("vector", lambda e: e.scalar_tensor_tensor(
                varb[:, sl], ssq[:, sl], 1.0 / D, msq[:, sl], ALU.mult, ALU.subtract),
                waits=[sq_last, tok_msq], inc="v")
            tok_var = p.tok("v")
            p.op("scalar", lambda e: e.activation(rstd[:, sl], varb[:, sl], AF.Ln,
                                                  bias=np.float32(EPS)),
                 waits=[tok_var], inc="s")
            tok_ln = p.tok("s")
            p.op("scalar", lambda e: e.activation(rstd[:, sl], rstd[:, sl], AF.Exp,
                                                  scale=np.float32(-0.5)),
                 waits=[tok_ln], inc="s")
            tok_rstd = p.tok("s")
            apply_toks = []
            for tt in range(lo, hi):
                p.op("vector", lambda e, tt=tt: e.tensor_scalar(
                    res3[:, tt, :], res3[:, tt, :], mall[:, tt:tt + 1],
                    rstd[:, tt:tt + 1], ALU.subtract, ALU.mult),
                    waits=[tok_rstd], inc="v")
                apply_toks.append(p.tok("v"))
            return apply_toks

        def emit_wout_load():
            p.op("gpsimd", lambda e: e.dma_start(
                out=wdq[:, 0:2, :, :].rearrange("p a b n -> p (a b) n"),
                in_=wo_d.rearrange("(b p) n -> p b n", p=128)),
                waits=[st["wd_free"][0], st["wd_free"][1]], inc="dwo", amt=16)
            st["wo_loaded"] = p.tok("dwo")

        def emit_attnout_resadd(s_i, ln1_stats=False):
            resadd_toks = []
            for tt in range(NT):
                pi = pnext()
                def aomm(e, tt=tt, pi=pi):
                    for nb in range(2):
                        for db in range(DB):
                            w_src = woA if db < 4 else woB
                            e.matmul(P[pi][:, nb * 512:(nb + 1) * 512],
                                     ofm[:, db, tt * 128:(tt + 1) * 128],
                                     w_src[:, db % 4, nb * 512:(nb + 1) * 512],
                                     start=(db == 0), stop=False)
                        ins = e.matmul(P[pi][:, nb * 512:(nb + 1) * 512],
                                       ones_r[0:1, 0:128],
                                       bo_r[0:1, nb * 512:(nb + 1) * 512],
                                       start=False, stop=True)
                    return ins
                p.op("tensor", aomm,
                     waits=[st[f"ofm_ready_{tt}"], st["P_free"][pi], st["wo_loaded"]],
                     inc="t")
                tok_ao = p.tok("t")
                if tt == NT - 1:
                    st["ofm_free"] = tok_ao
                    st["wo_free"] = tok_ao
                    st["wd_free"][0] = tok_ao
                    st["wd_free"][1] = tok_ao
                p.op("vector", lambda e, tt=tt, pi=pi: e.tensor_tensor(
                    res3[:, tt, :], res3[:, tt, :], P[pi][:], ALU.add),
                    waits=[tok_ao, st["tok_xload"]], inc="v")
                tok_resa = p.tok("v")
                st["P_free"][pi] = tok_resa
                resadd_toks.append(tok_resa)
                if ln1_stats:
                    emit_ln_stats(tt, tok_resa)
            return resadd_toks

        def emit_y_router_half(s_i, hf, y_toks_half):
            yfm_ready = []
            for k, tt in enumerate(range(hf * 4, hf * 4 + 4)):
                b = tt % 2
                p.op("scalar", lambda e, tt=tt, b=b: e.copy(xbf[:, b, :], res3[:, tt, :]),
                     waits=[y_toks_half[k], st["xbf_free"][b], st["sq_last"]], inc="s")
                tok_ybf = p.tok("s")

                def ytr(e, b=b):
                    for blk in range(DB):
                        ins = e.transpose(pT[:, blk * 128:(blk + 1) * 128],
                                          xbf[:, b, blk * 128:(blk + 1) * 128], idb[:])
                    return ins
                p.op("tensor", ytr, waits=[tok_ybf, st["pT_free"], st["yfm_free"]], inc="t")
                tok_ytr = p.tok("t")
                st["xbf_free"][b] = tok_ytr
                p.op("scalar", lambda e, tt=tt: e.copy(
                    yfm[:, :, tt * 128:(tt + 1) * 128],
                    pT[:].rearrange("p (b c) -> p b c", c=128)),
                    waits=[tok_ytr], inc="s")
                st["pT_free"] = p.tok("s")
                yfm_ready.append(st["pT_free"])
            st["yfm_half"][hf] = yfm_ready[-1]
            ltm_toks = []
            for k, tt in enumerate(range(hf * 4, hf * 4 + 4)):
                pi = pnext()
                def lgmm(e, tt=tt, pi=pi):
                    for db in range(DB):
                        ins = e.matmul(P[pi][:, 0:E],
                                       yfm[:, db, tt * 128:(tt + 1) * 128],
                                       rt_s[:, db, :],
                                       start=(db == 0), stop=(db == DB - 1))
                    return ins
                p.op("tensor", lgmm, waits=[yfm_ready[k], st["P_free"][pi]], inc="t")
                tok_lg = p.tok("t")
                p.op("scalar", lambda e, tt=tt, pi=pi: e.copy(
                    ltm[:, tt * E:(tt + 1) * E], P[pi][:, 0:E]),
                    waits=[tok_lg], inc="s")
                st["P_free"][pi] = p.tok("s")
                ltm_toks.append(st["P_free"][pi])
            NH = 4
            cs = slice(hf * 4 * E, (hf * 4 + 4) * E)
            ss = slice(hf * 4, hf * 4 + 4)
            lv = lambda tns: tns[:, cs].rearrange("p (t e) -> p t e", e=E)
            sv = lambda tns: tns[:, ss]
            p.op("vector", lambda e: e.reduce_max(sv(p1b), lv(ltm), axis=AX.X),
                 waits=[ltm_toks[-1]], inc="v")
            t1v = p.tok("v")
            p.op("vector", lambda e: e.tensor_tensor(
                lv(m2b), lv(ltm), sv(p1b).unsqueeze(2).broadcast_to([128, NH, E]),
                ALU.subtract), waits=[t1v], inc="v")
            tok_lsh = p.tok("v")
            p.op("scalar", lambda e: e.activation(probs[:, cs], m2b[:, cs], AF.Exp),
                 waits=[tok_lsh], inc="s")
            tok_pexp = p.tok("s")
            p.op("vector", lambda e: e.reduce_sum(sv(p2b), lv(probs), axis=AX.X),
                 waits=[tok_pexp], inc="v")
            t2v = p.tok("v")
            p.op("vector", lambda e: e.reciprocal(sv(d21), sv(p2b)), waits=[t2v], inc="v")
            t3v = p.tok("v")
            p.op("vector", lambda e: e.tensor_tensor(
                lv(probs), lv(probs), sv(d21).unsqueeze(2).broadcast_to([128, NH, E]),
                ALU.mult), waits=[t3v], inc="v")
            t4v = p.tok("v")
            p.op("vector", lambda e: e.reduce_max(sv(p1b), lv(probs), axis=AX.X),
                 waits=[t4v], inc="v")
            t5v = p.tok("v")
            p.op("vector", lambda e: e.tensor_tensor(
                lv(m1b), lv(probs), sv(p1b).unsqueeze(2).broadcast_to([128, NH, E]),
                ALU.is_equal), waits=[t5v], inc="v")
            t6v = p.tok("v")
            p.op("vector", lambda e: e.scalar_tensor_tensor(
                lv(m2b), lv(m1b), -1e30, lv(probs), ALU.mult, ALU.add),
                waits=[t6v], inc="v")
            t7v = p.tok("v")
            p.op("vector", lambda e: e.reduce_max(sv(p2b), lv(m2b), axis=AX.X),
                 waits=[t7v], inc="v")
            t8v = p.tok("v")
            p.op("vector", lambda e: e.tensor_tensor(sv(d21), sv(p2b), sv(p1b),
                                                     ALU.subtract), waits=[t8v], inc="v")
            tok_d21 = p.tok("v")
            p.op("scalar", lambda e: e.activation(texp[:, ss], d21[:, ss], AF.Exp),
                 waits=[tok_d21], inc="s")
            tok_texp = p.tok("s")
            p.op("vector", lambda e: e.tensor_scalar_add(sv(n1b), sv(texp), 1.0),
                 waits=[tok_texp], inc="v")
            t9v = p.tok("v")
            p.op("vector", lambda e: e.tensor_tensor(
                lv(m2b), lv(probs), sv(p2b).unsqueeze(2).broadcast_to([128, NH, E]),
                ALU.is_equal), waits=[t8v], inc="v")
            t10v = p.tok("v")
            p.op("vector", lambda e: e.reciprocal(sv(n1b), sv(n1b)), waits=[t9v], inc="v")
            t11v = p.tok("v")
            p.op("vector", lambda e: e.scalar_tensor_tensor(
                lv(m2b), lv(m1b), -1.0, lv(m2b), ALU.mult, ALU.add),
                waits=[t10v], inc="v")
            t12v = p.tok("v")
            p.op("vector", lambda e: e.tensor_tensor(sv(n2b), sv(texp), sv(n1b), ALU.mult),
                 waits=[t11v], inc="v")
            t13v = p.tok("v")
            p.op("vector", lambda e: e.tensor_relu(m2b[:, cs], m2b[:, cs]),
                 waits=[t12v], inc="v")
            t14v = p.tok("v")
            p.op("vector", lambda e: e.tensor_tensor(
                lv(wfull), lv(m1b), sv(n1b).unsqueeze(2).broadcast_to([128, NH, E]),
                ALU.mult), waits=[t11v], inc="v")
            t15v = p.tok("v")
            p.op("vector", lambda e: e.tensor_tensor(
                lv(m2b), lv(m2b), sv(n2b).unsqueeze(2).broadcast_to([128, NH, E]),
                ALU.mult), waits=[t13v, t14v], inc="v")
            t16v = p.tok("v")
            p.op("vector", lambda e: e.tensor_tensor(wfull[:, cs], wfull[:, cs],
                                                     m2b[:, cs], ALU.add),
                 waits=[t15v, t16v], inc="v")
            st["tok_w_half"][hf] = p.tok("v")

        def emit_ffn_block(s_i, e_i, fh, gA, gB, g_toks, u_toks, d_toks, ln2=False):
            par = s_i % 2
            pump(gB, 2)
            for j in range(8):
                fb = fh * 8 + j
                i8 = fb // 2
                col = (fb % 2) * 128
                bw = (e_i * 8 + i8) % 2
                nxt = i8 + 1
                if nxt < 8 and nxt not in g_toks:
                    emit_gu(e_i, nxt, g_toks, u_toks)
                pg = pnext()
                def gmm(e, bw=bw, col=col, pg=pg):
                    for nb in range(2):
                        for db in range(DB):
                            ins = e.matmul(P[pg][:, nb * 512:(nb + 1) * 512],
                                           wgq[:, bw, db, col:col + 128],
                                           yfm[:, db, nb * 512:(nb + 1) * 512],
                                           start=(db == 0), stop=(db == DB - 1))
                    return ins
                p.op("tensor", gmm,
                     waits=[g_toks[i8], st["yfm_half"][1], st["P_free"][pg]], inc="t")
                tok_g = p.tok("t")
                if fb % 2 == 1:
                    st["wg_free"][bw] = tok_g
                p.op("scalar", lambda e, pg=pg: e.activation(fg[:], P[pg][:], AF.Silu),
                     waits=[tok_g, st["fg_free"][0]], inc="s")
                tok_silu = p.tok("s")
                st["P_free"][pg] = tok_silu
                pu = pnext()
                def umm(e, bw=bw, col=col, pu=pu):
                    for nb in range(2):
                        for db in range(DB):
                            ins = e.matmul(P[pu][:, nb * 512:(nb + 1) * 512],
                                           wuq[:, bw, db, col:col + 128],
                                           yfm[:, db, nb * 512:(nb + 1) * 512],
                                           start=(db == 0), stop=(db == DB - 1))
                    return ins
                p.op("tensor", umm, waits=[u_toks[i8], st["P_free"][pu]], inc="t")
                tok_u = p.tok("t")
                if fb % 2 == 1:
                    st["wu_free"][bw] = tok_u
                p.op("vector", lambda e, j=j, pu=pu, par=par: e.tensor_tensor(
                    bufP[par][:, j, :], fg[:], P[pu][:], ALU.mult),
                    waits=[tok_silu, tok_u,
                           st["bufP_free"][par] if fb == 0 else ("t", 0)],
                    inc="v")
                tok_fuse = p.tok("v")
                st["fg_free"][0] = tok_fuse
                st["P_free"][pu] = tok_fuse
                pump(gB, 2)
            tok_fuse_all = p.tok("v")
            for tt in range(NT):
                pd = pnext()
                def dmm(e, tt=tt, fh=fh, pd=pd, e_i=e_i, par=par):
                    for nn in range(2):
                        for j in range(8):
                            q4 = fh * 2 + j // 4
                            bw = (e_i * 4 + q4) % 3
                            ins = e.matmul(P[pd][:, nn * 512:(nn + 1) * 512],
                                           bufP[par][:, j, tt * 128:(tt + 1) * 128],
                                           wdq[:, bw, j % 4, nn * 512:(nn + 1) * 512],
                                           start=(j == 0), stop=(j == 7))
                    return ins
                p.op("tensor", dmm,
                     waits=[tok_fuse_all, d_toks[fh * 2], d_toks[fh * 2 + 1],
                            st["P_free"][pd]], inc="t")
                tok_d = p.tok("t")
                if tt == NT - 1:
                    st["bufP_free"][par] = tok_d
                    for q4 in (fh * 2, fh * 2 + 1):
                        st["wd_free"][(e_i * 4 + q4) % 3] = tok_d
                    if e_i == E - 1:
                        st["yfm_free"] = p.tok("t")
                p.op("vector", lambda e, tt=tt, pd=pd, e_i=e_i: e.scalar_tensor_tensor(
                    res3[:, tt, :], P[pd][:], wfull[:, tt * E + e_i:tt * E + e_i + 1],
                    res3[:, tt, :], ALU.mult, ALU.add),
                    waits=[tok_d, st["tok_w_half"][tt // 4]], inc="v")
                st["dacc"][tt] = p.tok("v")
                st["P_free"][pd] = st["dacc"][tt]
                if ln2:
                    emit_ln_stats(tt, st["dacc"][tt])
                pump(gB, 2)

        def emit_gu(e_i, i8, g_toks, u_toks):
            bw = (e_i * 8 + i8) % 2
            p.op("gpsimd", lambda e, e_i=e_i, i8=i8, bw=bw: e.dma_start(
                out=wgq[:, bw, :, :],
                in_=wg_d[e_i].rearrange("(b p) f -> p b f", p=128)[:, :, i8 * 256:(i8 + 1) * 256]),
                waits=[st["wg_free"][bw]], inc=f"dg{bw}", amt=16)
            g_toks[i8] = p.tok(f"dg{bw}")
            p.op("gpsimd", lambda e, e_i=e_i, i8=i8, bw=bw: e.dma_start(
                out=wuq[:, bw, :, :],
                in_=wu_d[e_i].rearrange("(b p) f -> p b f", p=128)[:, :, i8 * 256:(i8 + 1) * 256]),
                waits=[st["wu_free"][bw]], inc=f"du{bw}", amt=16)
            u_toks[i8] = p.tok(f"du{bw}")

        def emit_wd(e_i, q4, d_toks):
            bw = (e_i * 4 + q4) % 3
            p.op("gpsimd", lambda e, e_i=e_i, q4=q4, bw=bw: e.dma_start(
                out=wdq[:, bw, :, :],
                in_=wd_d[e_i].rearrange("(b p) d -> p b d", p=128)[:, q4 * 4:(q4 + 1) * 4, :]),
                waits=[st["wd_free"][bw]], inc=f"dd{bw}", amt=16)
            d_toks[q4] = p.tok(f"dd{bw}")

        def attention_gens(s_i):
            gA = deque()
            for tt in range(NT):
                gA.append(gen_xcast(s_i, tt))
            gA.append(gen_lat(s_i))
            gB = deque()
            gB.append(gen_qkv(s_i, 0))
            for tt in range(NT):
                if tt + 1 < NT:
                    gB.append(gen_qkv(s_i, tt + 1))
                gB.append(gen_attvec(s_i, tt))
            return gA, gB

        st["dacc"] = [None] * NT
        gA, gB = attention_gens(0)
        flush(gA)
        flush(gB)
        emit_wout_load()
        rows0 = slice(0, S)
        p.op("sync", lambda e, r=rows0: e.dma_start(
            out=res3[:], in_=x_d[r, :].rearrange("(tt p) d -> p tt d", p=128)),
            waits=[tok_bdnc], inc="dr", amt=16)
        st["tok_xload"] = p.tok("dr")
        resadd = emit_attnout_resadd(0, ln1_stats=True)
        for hf in range(2):
            y_half = emit_ln_finish(hf * 4, hf * 4 + 4)
            emit_y_router_half(0, hf, y_half)

        for s_i in range(nsub):
            st["dacc"] = [None] * NT
            if s_i + 1 < nsub:
                gA, gB = attention_gens(s_i + 1)
                flush(gA)
            else:
                gB = deque()
            for e_i in range(E):
                g_toks = {}
                u_toks = {}
                d_toks = {}
                emit_gu(e_i, 0, g_toks, u_toks)
                emit_gu(e_i, 1, g_toks, u_toks)
                for fh in range(2):
                    emit_wd(e_i, fh * 2, d_toks)
                    emit_wd(e_i, fh * 2 + 1, d_toks)
                    emit_ffn_block(s_i, e_i, fh, gA, gB, g_toks, u_toks, d_toks,
                                   ln2=(e_i == E - 1 and fh == 1))
            flush(gB)
            if s_i + 1 < nsub:
                emit_wout_load()
            out_toks = emit_ln_finish(0, NT)
            rows = slice(s_i * S, (s_i + 1) * S)
            p.op("sync", lambda e, r=rows: e.dma_start(
                out=out_d[r, :].rearrange("(tt p) d -> p tt d", p=128), in_=res3[:]),
                waits=[out_toks[-1]], inc="so", amt=16)
            st["res_free"] = p.tok("so")
            if s_i + 1 < nsub:
                rows2 = slice((s_i + 1) * S, (s_i + 2) * S)
                p.op("sync", lambda e, r=rows2: e.dma_start(
                    out=res3[:], in_=x_d[r, :].rearrange("(tt p) d -> p tt d", p=128)),
                    waits=[st["res_free"]], inc="dr", amt=16)
                st["tok_xload"] = p.tok("dr")
                resadd = emit_attnout_resadd(s_i + 1, ln1_stats=True)
                for hf in range(2):
                    y_half = emit_ln_finish(hf * 4, hf * 4 + 4)
                    emit_y_router_half(s_i + 1, hf, y_half)

        p.op("sync", lambda e: e.nop(), waits=[st["res_free"]])

        with nc.Block() as block:
            @block.sync
            def _(eng):
                replay(p, "sync", eng, sems)

            @block.gpsimd
            def _(eng):
                replay(p, "gpsimd", eng, sems)

            @block.tensor
            def _(eng):
                replay(p, "tensor", eng, sems)

            @block.vector
            def _(eng):
                replay(p, "vector", eng, sems)

            @block.scalar
            def _(eng):
                replay(p, "scalar", eng, sems)

    return nc


def make_inputs(inp, ntok_per_core, cores):
    x = np.ascontiguousarray(np.asarray(inp["L0_attn_x"], dtype=np.float32).reshape(-1, D))
    perm = np.empty(2 * D, dtype=np.int64)
    for h in range(H):
        for d in range(HD):
            perm[h * HD + d] = h * 128 + d
    for d in range(HD):
        for g in range(H):
            perm[D + d * H + g] = g * 128 + 64 + d
    wup = np.ascontiguousarray(np.asarray(inp["L0_attn_Wup"], dtype=np.float32)[:, perm])
    bup = np.ascontiguousarray(np.asarray(inp["L0_attn_bup"], dtype=np.float32)[perm])
    common = dict(
        wq=np.ascontiguousarray(np.asarray(inp["L0_attn_Wq"], dtype=np.float32)),
        bq=np.ascontiguousarray(np.asarray(inp["L0_attn_bq"], dtype=np.float32)),
        wdn=np.ascontiguousarray(np.asarray(inp["L0_attn_Wdown"], dtype=np.float32)),
        bdn=np.ascontiguousarray(np.asarray(inp["L0_attn_bdown"], dtype=np.float32)),
        wup=wup, bup=bup,
        wo=np.ascontiguousarray(np.asarray(inp["L0_attn_Wout"], dtype=np.float32)),
        bo=np.ascontiguousarray(np.asarray(inp["L0_attn_bout"], dtype=np.float32)),
        rt=np.ascontiguousarray(np.asarray(inp["L0_ffn_router_kernel_DE"], dtype=np.float32)),
        wg=np.ascontiguousarray(np.asarray(inp["L0_ffn_kernel_gating_EDF"], dtype=np.float32)),
        wu=np.ascontiguousarray(np.asarray(inp["L0_ffn_kernel_up_EDF"], dtype=np.float32)),
        wd=np.ascontiguousarray(np.asarray(inp["L0_ffn_kernel_down_EFD"], dtype=np.float32)),
        idm=np.eye(128, dtype=np.float32),
    )
    in_maps = []
    for c in range(cores):
        m = dict(common)
        m["x0"] = np.ascontiguousarray(x[c * ntok_per_core:(c + 1) * ntok_per_core])
        in_maps.append(m)
    return in_maps


def kernel(**inputs):
    in_maps = make_inputs(inputs, 4096, NCORE)
    nc = build_program(nsub=4)
    res = run_bass_kernel_spmd(nc, in_maps, core_ids=list(range(NCORE)))
    out = np.concatenate([r["out"] for r in res.results], axis=0)
    return out.reshape(2, 16384, 1024)


# version 14 (final)
# speedup vs baseline: 1.0263x; 1.0263x over previous
import numpy as np
import ml_dtypes
import concourse.bass as bass
import concourse.mybir as mybir
from concourse.bass_utils import run_bass_kernel_spmd
from contextlib import ExitStack
from collections import deque

F32 = mybir.dt.float32
BF16 = mybir.dt.bfloat16
AF = mybir.ActivationFunctionType
ALU = mybir.AluOpType
AX = mybir.AxisListType

NCORE = 8
S = 1024
NT = S // 128
D = 1024
DB = D // 128
LATD = 256
LB = LATD // 128
FF = 2048
E = 8
H = 16
HD = 64
EPS = 1e-5

ENGINES = ("sync", "gpsimd", "tensor", "vector", "scalar")


class Plan:
    def __init__(self):
        self.ops = {e: [] for e in ENGINES}
        self.cnt = {}

    def sem(self, name):
        self.cnt[name] = 0

    def op(self, eng, fn, waits=(), inc=None, amt=1):
        ws = tuple((s, v) for (s, v) in waits if v > 0)
        self.ops[eng].append((ws, fn, inc, amt))
        if inc is not None:
            self.cnt[inc] += amt
            return (inc, self.cnt[inc])
        return None

    def tok(self, name):
        return (name, self.cnt[name])


def replay(plan, name, eng, sems):
    waited = {}
    for ws, fn, inc, amt in plan.ops[name]:
        for s, v in ws:
            if waited.get(s, 0) >= v:
                continue
            eng.wait_ge(sems[s], v)
            waited[s] = v
        ins = fn(eng)
        if inc is not None:
            ins.then_inc(sems[inc], amt)


def build_program(nsub):
    ntok = nsub * S
    nc = bass.Bass("TRN2", target_bir_lowering=False)
    x_d = nc.dram_tensor("x0", [ntok, D], F32, kind="ExternalInput").ap()
    wq_d = nc.dram_tensor("wq", [D, D], F32, kind="ExternalInput").ap()
    bq_d = nc.dram_tensor("bq", [D], F32, kind="ExternalInput").ap()
    wdn_d = nc.dram_tensor("wdn", [D, LATD], F32, kind="ExternalInput").ap()
    bdn_d = nc.dram_tensor("bdn", [LATD], F32, kind="ExternalInput").ap()
    wup_d = nc.dram_tensor("wup", [LATD, 2 * D], F32, kind="ExternalInput").ap()
    bup_d = nc.dram_tensor("bup", [2 * D], F32, kind="ExternalInput").ap()
    wo_d = nc.dram_tensor("wo", [D, D], F32, kind="ExternalInput").ap()
    bo_d = nc.dram_tensor("bo", [D], F32, kind="ExternalInput").ap()
    rt_d = nc.dram_tensor("rt", [D, E], F32, kind="ExternalInput").ap()
    wg_d = nc.dram_tensor("wg", [E, D, FF], F32, kind="ExternalInput").ap()
    wu_d = nc.dram_tensor("wu", [E, D, FF], F32, kind="ExternalInput").ap()
    wd_d = nc.dram_tensor("wd", [E, FF, D], F32, kind="ExternalInput").ap()
    id_d = nc.dram_tensor("idm", [128, 128], F32, kind="ExternalInput").ap()
    out_d = nc.dram_tensor("out", [ntok, D], F32, kind="ExternalOutput").ap()

    p = Plan()
    SEMS = ("di", "dj", "dk", "dr", "dc0", "dc1", "dg0", "dg1", "du0", "du1",
            "dd0", "dd1", "dd2", "dwo", "t", "v", "s", "so")
    for sname in SEMS:
        p.sem(sname)

    with ExitStack() as ctx:
        en = ctx.enter_context
        wq_s = en(nc.sbuf_tensor("wq_s", [128, DB, D], BF16))
        wdn_s = en(nc.sbuf_tensor("wdn_s", [128, DB, LATD], BF16))
        wup_s = en(nc.sbuf_tensor("wup_s", [128, LB, 2 * D], BF16))
        rt_s = en(nc.sbuf_tensor("rt_s", [128, DB, E], BF16))
        bq_r = en(nc.sbuf_tensor("bq_r", [1, D], BF16))
        bup_r = en(nc.sbuf_tensor("bup_r", [1, 2 * D], BF16))
        bo_r = en(nc.sbuf_tensor("bo_r", [1, D], BF16))
        bdn_c = en(nc.sbuf_tensor("bdn_c", [128, LB], F32))
        idb = en(nc.sbuf_tensor("idb", [128, 128], BF16))
        idf8 = en(nc.sbuf_tensor("idf8", [8, 8], F32))
        ones_r = en(nc.sbuf_tensor("ones_r", [1, 128], BF16))
        res3 = en(nc.sbuf_tensor("res3", [128, NT, D], F32))
        bufP = [en(nc.sbuf_tensor(f"bufP{i}", [128, DB, S], BF16)) for i in range(2)]
        yfm = en(nc.sbuf_tensor("yfm", [128, DB, S], BF16))
        ofm = en(nc.sbuf_tensor("ofm", [128, DB, S], BF16))
        latfm = en(nc.sbuf_tensor("latfm", [128, LB, S], BF16))
        wdq = en(nc.sbuf_tensor("wdq", [128, 3, 4, D], BF16))
        wgq = en(nc.sbuf_tensor("wgq", [128, 2, DB, 256], BF16))
        wuq = en(nc.sbuf_tensor("wuq", [128, 2, DB, 256], BF16))
        qtm = en(nc.sbuf_tensor("qtm", [128, 2, D], BF16))
        kvtm = en(nc.sbuf_tensor("kvtm", [128, 2, 2 * D], BF16))
        tmp3 = en(nc.sbuf_tensor("tmp3", [128, 3, D], BF16))
        xbf = en(nc.sbuf_tensor("xbf", [128, 2, D], BF16))
        otm = en(nc.sbuf_tensor("otm", [128, D], BF16))
        fg = en(nc.sbuf_tensor("fg", [128, D], BF16))
        att = en(nc.sbuf_tensor("att", [128, H * H], F32))
        attp = en(nc.sbuf_tensor("attp", [128, H * H], BF16))
        amax = en(nc.sbuf_tensor("amax", [128, H], F32))
        asum = en(nc.sbuf_tensor("asum", [128, H], F32))
        arec = en(nc.sbuf_tensor("arec", [128, H], F32))
        ltm = en(nc.sbuf_tensor("ltm", [128, NT * E], F32))
        probs = en(nc.sbuf_tensor("probs", [128, NT * E], F32))
        m1b = en(nc.sbuf_tensor("m1b", [128, NT * E], F32))
        m2b = en(nc.sbuf_tensor("m2b", [128, NT * E], F32))
        wfull = en(nc.sbuf_tensor("wfull", [128, NT * E], F32))
        p1b = en(nc.sbuf_tensor("p1b", [128, NT], F32))
        p2b = en(nc.sbuf_tensor("p2b", [128, NT], F32))
        d21 = en(nc.sbuf_tensor("d21", [128, NT], F32))
        texp = en(nc.sbuf_tensor("texp", [128, NT], F32))
        n1b = en(nc.sbuf_tensor("n1b", [128, NT], F32))
        n2b = en(nc.sbuf_tensor("n2b", [128, NT], F32))
        ssum = en(nc.sbuf_tensor("ssum", [128, NT], F32))
        ssq = en(nc.sbuf_tensor("ssq", [128, NT], F32))
        mall = en(nc.sbuf_tensor("mall", [128, NT], F32))
        msq = en(nc.sbuf_tensor("msq", [128, NT], F32))
        varb = en(nc.sbuf_tensor("varb", [128, NT], F32))
        rstd = en(nc.sbuf_tensor("rstd", [128, NT], F32))
        P = [en(nc.psum_tensor(f"P{i}", [128, D], F32)) for i in range(3)]
        pT = en(nc.psum_tensor("pT", [128, D], BF16))
        pT2 = en(nc.psum_tensor("pT2", [128, D], BF16))

        sems = {s_: en(nc.semaphore(s_)) for s_ in SEMS}

        woA = wdq[:, 0, :, :]
        woB = wdq[:, 1, :, :]

        st = dict(
            P_free=[("s", 0)] * 3, rotP=0,
            pT_free=("s", 0), pT2_free=("s", 0),
            xbf_free=[("t", 0), ("t", 0)],
            qtm_free=[("v", 0), ("v", 0)], kv_free=[("v", 0), ("v", 0)],
            otm_free=("t", 0),
            fg_free=[("v", 0), ("v", 0)],
            yfm_half=[("s", 0), ("s", 0)], tok_w_half=[("v", 0), ("v", 0)],
            wg_free=[("t", 0), ("t", 0)], wu_free=[("t", 0), ("t", 0)],
            wd_free=[("t", 0), ("t", 0), ("t", 0)],
            bufP_free=[("t", 0), ("t", 0)],
            yfm_free=("t", 0), ofm_free=("t", 0), lat_free=("t", 0),
            res_free=("so", 0), wo_loaded=("dwo", 0), wo_free=("t", 0),
            prev_exp=("s", 0), tmp_free=("v", 0), sq_last=("s", 0), sum_last=("v", 0),
        )

        def pnext():
            i = st["rotP"] % 3
            st["rotP"] += 1
            return i

        p.op("sync", lambda e: e.dma_start(out=res3[0:LB, 0, 0:128], in_=bdn_d.rearrange("(c p) -> c p", p=128)), inc="dj", amt=16)
        p.op("sync", lambda e: e.dma_start(out=idf8[:], in_=id_d[0:8, 0:8]), inc="dj", amt=16)
        init_dj = p.tok("dj")
        p.op("gpsimd", lambda e: e.dma_start(out=idb[:], in_=id_d), inc="dk", amt=16)
        init_dk = p.tok("dk")

        def emit_winit():
            for dst, src_ in (
                (wq_s[:], wq_d.rearrange("(b p) n -> p b n", p=128)),
                (wdn_s[:], wdn_d.rearrange("(b p) n -> p b n", p=128)),
                (wup_s[:], wup_d.rearrange("(b p) n -> p b n", p=128)),
                (rt_s[:], rt_d.rearrange("(b p) n -> p b n", p=128)),
                (bq_r[:], bq_d[None, :]),
                (bup_r[:], bup_d[None, :]),
                (bo_r[:], bo_d[None, :]),
            ):
                p.op("gpsimd", lambda e, d=dst, s_=src_: e.dma_start(out=d, in_=s_),
                     inc="di", amt=16)
            return p.tok("di")
        p.op("vector", lambda e: e.memset(ones_r[:], 1.0), inc="v")
        tok_ones = p.tok("v")
        p.op("tensor", lambda e: e.transpose(P[0][:, 0:LB], res3[0:LB, 0, 0:128], idf8[0:LB, 0:LB]),
             waits=[init_dj], inc="t")
        p.op("scalar", lambda e: e.copy(bdn_c[:], P[0][:, 0:LB]), waits=[p.tok("t")], inc="s")
        tok_bdnc = p.tok("s")
        st["P_free"][0] = tok_bdnc

        def gen_xcast(s_i, tt):
            par = s_i % 2
            b = tt % 2
            r0 = s_i * S + tt * 128
            p.op("gpsimd", lambda e, r=r0, b=b: e.dma_start(
                out=xbf[:, b, :], in_=x_d[r:r + 128, :]),
                waits=[st["xbf_free"][b]], inc=f"dc{b}", amt=16)
            tok_cast = p.tok(f"dc{b}")
            yield

            def xtr(e, b=b):
                for blk in range(DB):
                    ins = e.transpose(pT[:, blk * 128:(blk + 1) * 128],
                                      xbf[:, b, blk * 128:(blk + 1) * 128], idb[:])
                return ins
            p.op("tensor", xtr, waits=[tok_cast, st["pT_free"], init_dk,
                                       st["bufP_free"][par]], inc="t")
            tok_xtr = p.tok("t")
            st["xbf_free"][b] = tok_xtr
            yield
            p.op("scalar", lambda e, tt=tt, par=par: e.copy(
                bufP[par][:, :, tt * 128:(tt + 1) * 128],
                pT[:].rearrange("p (b c) -> p b c", c=128)),
                waits=[tok_xtr], inc="s")
            st["pT_free"] = p.tok("s")
            st[f"xfm_ready_{tt}"] = st["pT_free"]
            yield

        def gen_lat(s_i):
            par = s_i % 2
            for lb in range(LB):
                pi = pnext()
                def latmm(e, lb=lb, pi=pi, par=par):
                    for nb in range(2):
                        for db in range(DB):
                            ins = e.matmul(P[pi][:, nb * 512:(nb + 1) * 512],
                                           wdn_s[:, db, lb * 128:(lb + 1) * 128],
                                           bufP[par][:, db, nb * 512:(nb + 1) * 512],
                                           start=(db == 0), stop=(db == DB - 1))
                    return ins
                p.op("tensor", latmm,
                     waits=[st["xfm_ready_7"], st["P_free"][pi], st["lat_free"], tok_bdnc,
                            init_di],
                     inc="t")
                tok_lat = p.tok("t")
                yield
                p.op("scalar", lambda e, lb=lb, pi=pi: e.activation(
                    latfm[:, lb, :], P[pi][:], AF.Identity, bias=bdn_c[:, lb:lb + 1]),
                    waits=[tok_lat], inc="s")
                st["P_free"][pi] = p.tok("s")
                st["lat_ready"] = p.tok("s")
                yield

        def gen_qkv(s_i, tt):
            par = s_i % 2
            pi = pnext()
            def qmm(e, tt=tt, pi=pi, par=par):
                for nb in range(2):
                    for db in range(DB):
                        e.matmul(P[pi][:, nb * 512:(nb + 1) * 512],
                                 bufP[par][:, db, tt * 128:(tt + 1) * 128],
                                 wq_s[:, db, nb * 512:(nb + 1) * 512],
                                 start=(db == 0), stop=False)
                    ins = e.matmul(P[pi][:, nb * 512:(nb + 1) * 512],
                                   ones_r[0:1, 0:128],
                                   bq_r[0:1, nb * 512:(nb + 1) * 512],
                                   start=False, stop=True)
                return ins
            p.op("tensor", qmm,
                 waits=[st[f"xfm_ready_{tt}"], st["P_free"][pi], init_di, tok_ones],
                 inc="t")
            tok_q = p.tok("t")
            yield
            b = tt % 2
            p.op("scalar", lambda e, pi=pi, b=b: e.activation(
                qtm[:, b, :], P[pi][:], AF.Identity, scale=np.float32(0.125)),
                waits=[tok_q, st["qtm_free"][b]], inc="s")
            st[f"tok_qe_{tt}"] = p.tok("s")
            st["P_free"][pi] = st[f"tok_qe_{tt}"]
            yield
            for half in range(2):
                pi = pnext()
                def kvmm(e, tt=tt, half=half, pi=pi):
                    for nb in range(2):
                        c0 = half * D + nb * 512
                        for lb in range(LB):
                            e.matmul(P[pi][:, nb * 512:(nb + 1) * 512],
                                     latfm[:, lb, tt * 128:(tt + 1) * 128],
                                     wup_s[:, lb, c0:c0 + 512],
                                     start=(lb == 0), stop=False)
                        ins = e.matmul(P[pi][:, nb * 512:(nb + 1) * 512],
                                       ones_r[0:1, 0:128],
                                       bup_r[0:1, c0:c0 + 512],
                                       start=False, stop=True)
                    return ins
                p.op("tensor", kvmm, waits=[st["lat_ready"], st["P_free"][pi]], inc="t")
                tok_kv = p.tok("t")
                yield
                p.op("scalar", lambda e, half=half, pi=pi, b=b: e.copy(
                    kvtm[:, b, half * D:(half + 1) * D], P[pi][:]),
                    waits=[tok_kv, st["kv_free"][b]], inc="s")
                st["P_free"][pi] = p.tok("s")
                yield
            st[f"tok_kve_{tt}"] = p.tok("s")

        def gen_attvec(s_i, tt):
            b = tt % 2
            kview = kvtm[:, b, 0:D].rearrange("p (g d) -> p g d", d=HD)
            vview = kvtm[:, b, D:2 * D].rearrange("p (d g) -> p d g", g=H)
            tok_qe = st[f"tok_qe_{tt}"]
            tok_kve = st[f"tok_kve_{tt}"]
            tok_m = [None] * H
            tok_r = [None] * H

            def qkm(h):
                def f(e, h=h, b=b, kview=kview):
                    return e.tensor_tensor(
                        tmp3[:, h % 3, :].rearrange("p (g d) -> p g d", d=HD),
                        qtm[:, b, h * HD:(h + 1) * HD].unsqueeze(1).broadcast_to([128, H, HD]),
                        kview, ALU.mult)
                return f

            def qkr(h):
                def f(e, h=h):
                    return e.reduce_sum(
                        att[:, h * H:(h + 1) * H],
                        tmp3[:, h % 3, :].rearrange("p (g d) -> p g d", d=HD), axis=AX.X)
                return f

            def emit_m(h):
                w = [tok_qe, tok_kve, st["prev_exp"], st["tmp_free"]] if h == 0 else (
                    [tok_r[h - 3]] if h >= 3 else [])
                p.op("vector", qkm(h), waits=w, inc="v")
                tok_m[h] = p.tok("v")

            def emit_r(h):
                p.op("vector", qkr(h), waits=[tok_m[h]], inc="v")
                tok_r[h] = p.tok("v")

            emit_m(0)
            yield
            emit_m(1)
            yield
            for h in range(2, H):
                emit_m(h)
                emit_r(h - 2)
                yield
            emit_r(H - 2)
            emit_r(H - 1)
            yield
            p.op("vector", lambda e: e.reduce_max(
                amax[:], att[:].rearrange("p (h g) -> p h g", g=H), axis=AX.X),
                waits=[tok_r[H - 1]], inc="v")
            tok_rmax = p.tok("v")
            p.op("vector", lambda e: e.tensor_tensor(
                att[:].rearrange("p (h g) -> p h g", g=H),
                att[:].rearrange("p (h g) -> p h g", g=H),
                amax[:].unsqueeze(2).broadcast_to([128, H, H]), ALU.subtract),
                waits=[tok_rmax], inc="v")
            tok_attm = p.tok("v")
            st["qtm_free"][b] = tok_attm
            yield
            p.op("scalar", lambda e: e.activation(attp[:], att[:], AF.Exp),
                 waits=[tok_attm], inc="s")
            tok_exp = p.tok("s")
            st["prev_exp"] = tok_exp
            yield
            p.op("vector", lambda e: e.reduce_sum(
                asum[:], attp[:].rearrange("p (h g) -> p h g", g=H), axis=AX.X),
                waits=[tok_exp], inc="v")
            tok_esum = p.tok("v")
            p.op("vector", lambda e: e.reciprocal(arec[:], asum[:]),
                 waits=[tok_esum], inc="v")
            tok_arec = p.tok("v")
            p.op("vector", lambda e: e.tensor_tensor(
                attp[:].rearrange("p (h g) -> p h g", g=H),
                attp[:].rearrange("p (h g) -> p h g", g=H),
                arec[:].unsqueeze(2).broadcast_to([128, H, H]), ALU.mult),
                waits=[tok_arec], inc="v")
            tok_probs = p.tok("v")
            yield

            tok_am = [None] * H
            tok_ar = [None] * H

            def avm(h):
                def f(e, h=h, vview=vview):
                    return e.tensor_tensor(
                        tmp3[:, h % 3, :].rearrange("p (d g) -> p d g", g=H),
                        attp[:, h * H:(h + 1) * H].unsqueeze(1).broadcast_to([128, HD, H]),
                        vview, ALU.mult)
                return f

            def avr(h):
                def f(e, h=h):
                    with e.bass.allow_low_precision(reason="bf16 matmul operand"):
                        return e.reduce_sum(
                            otm[:, h * HD:(h + 1) * HD],
                            tmp3[:, h % 3, :].rearrange("p (d g) -> p d g", g=H), axis=AX.X)
                return f

            def emit_am(h):
                w = [tok_probs, st["otm_free"]] if h == 0 else (
                    [tok_ar[h - 3]] if h >= 3 else [])
                p.op("vector", avm(h), waits=w, inc="v")
                tok_am[h] = p.tok("v")

            def emit_ar(h):
                p.op("vector", avr(h), waits=[tok_am[h]], inc="v")
                tok_ar[h] = p.tok("v")

            emit_am(0)
            yield
            emit_am(1)
            yield
            for h in range(2, H):
                emit_am(h)
                emit_ar(h - 2)
                yield
            emit_ar(H - 2)
            emit_ar(H - 1)
            tok_o = tok_ar[H - 1]
            st["kv_free"][b] = tok_o
            st["tmp_free"] = tok_o
            yield
            yield
            yield
            def otr(e):
                for blk in range(DB):
                    ins = e.transpose(pT2[:, blk * 128:(blk + 1) * 128],
                                      otm[:, blk * 128:(blk + 1) * 128], idb[:])
                return ins
            p.op("tensor", otr, waits=[tok_o, st["pT2_free"],
                                       st["ofm_free"] if tt == 0 else ("t", 0)],
                 inc="t")
            tok_otr = p.tok("t")
            st["otm_free"] = tok_otr
            yield
            p.op("scalar", lambda e, tt=tt: e.copy(
                ofm[:, :, tt * 128:(tt + 1) * 128],
                pT2[:].rearrange("p (b c) -> p b c", c=128)),
                waits=[tok_otr], inc="s")
            st["pT2_free"] = p.tok("s")
            st[f"ofm_ready_{tt}"] = st["pT2_free"]
            yield

        def pump(gens, n):
            k = 0
            while gens and k < n:
                try:
                    next(gens[0])
                    k += 1
                except StopIteration:
                    gens.popleft()

        def flush(gens):
            while gens:
                try:
                    next(gens[0])
                except StopIteration:
                    gens.popleft()

        def emit_ln_stats(tt, wait_tok):
            def sq(e, tt=tt):
                with e.bass.allow_low_precision(reason="scratch write; accum is fp32"):
                    return e.activation(xbf[:, 0, :], res3[:, tt, :], AF.Square,
                                        accum_out=ssq[:, tt:tt + 1])
            p.op("scalar", sq, waits=[wait_tok, st["sq_last"], st["xbf_free"][0]], inc="s")
            st["sq_last"] = p.tok("s")
            p.op("vector", lambda e, tt=tt: e.reduce_sum(
                ssum[:, tt:tt + 1], res3[:, tt, :], axis=AX.X),
                waits=[wait_tok], inc="v")
            st["sum_last"] = p.tok("v")

        def emit_ln_finish(lo, hi):
            sq_last = st["sq_last"]
            sl = slice(lo, hi)
            p.op("vector", lambda e: e.tensor_scalar_mul(mall[:, sl], ssum[:, sl], 1.0 / D),
                 waits=[st["sum_last"]], inc="v")
            tok_mall = p.tok("v")
            p.op("vector", lambda e: e.tensor_tensor(msq[:, sl], mall[:, sl], mall[:, sl], ALU.mult),
                 waits=[tok_mall], inc="v")
            tok_msq = p.tok("v")
            p.op("vector", lambda e: e.scalar_tensor_tensor(
                varb[:, sl], ssq[:, sl], 1.0 / D, msq[:, sl], ALU.mult, ALU.subtract),
                waits=[sq_last, tok_msq], inc="v")
            tok_var = p.tok("v")
            p.op("scalar", lambda e: e.activation(rstd[:, sl], varb[:, sl], AF.Ln,
                                                  bias=np.float32(EPS)),
                 waits=[tok_var], inc="s")
            tok_ln = p.tok("s")
            p.op("scalar", lambda e: e.activation(rstd[:, sl], rstd[:, sl], AF.Exp,
                                                  scale=np.float32(-0.5)),
                 waits=[tok_ln], inc="s")
            tok_rstd = p.tok("s")
            apply_toks = []
            for tt in range(lo, hi):
                p.op("vector", lambda e, tt=tt: e.tensor_scalar(
                    res3[:, tt, :], res3[:, tt, :], mall[:, tt:tt + 1],
                    rstd[:, tt:tt + 1], ALU.subtract, ALU.mult),
                    waits=[tok_rstd], inc="v")
                apply_toks.append(p.tok("v"))
            return apply_toks

        def emit_wout_load():
            p.op("gpsimd", lambda e: e.dma_start(
                out=wdq[:, 0:2, :, :].rearrange("p a b n -> p (a b) n"),
                in_=wo_d.rearrange("(b p) n -> p b n", p=128)),
                waits=[st["wd_free"][0], st["wd_free"][1]], inc="dwo", amt=16)
            st["wo_loaded"] = p.tok("dwo")

        def emit_attnout_resadd(s_i, ln1_stats=False):
            resadd_toks = []
            for tt in range(NT):
                pi = pnext()
                def aomm(e, tt=tt, pi=pi):
                    for nb in range(2):
                        for db in range(DB):
                            w_src = woA if db < 4 else woB
                            e.matmul(P[pi][:, nb * 512:(nb + 1) * 512],
                                     ofm[:, db, tt * 128:(tt + 1) * 128],
                                     w_src[:, db % 4, nb * 512:(nb + 1) * 512],
                                     start=(db == 0), stop=False)
                        ins = e.matmul(P[pi][:, nb * 512:(nb + 1) * 512],
                                       ones_r[0:1, 0:128],
                                       bo_r[0:1, nb * 512:(nb + 1) * 512],
                                       start=False, stop=True)
                    return ins
                p.op("tensor", aomm,
                     waits=[st[f"ofm_ready_{tt}"], st["P_free"][pi], st["wo_loaded"]],
                     inc="t")
                tok_ao = p.tok("t")
                if tt == NT - 1:
                    st["ofm_free"] = tok_ao
                    st["wo_free"] = tok_ao
                    st["wd_free"][0] = tok_ao
                    st["wd_free"][1] = tok_ao
                p.op("vector", lambda e, tt=tt, pi=pi: e.tensor_tensor(
                    res3[:, tt, :], res3[:, tt, :], P[pi][:], ALU.add),
                    waits=[tok_ao, st["tok_xload"]], inc="v")
                tok_resa = p.tok("v")
                st["P_free"][pi] = tok_resa
                resadd_toks.append(tok_resa)
                if ln1_stats:
                    emit_ln_stats(tt, tok_resa)
            return resadd_toks

        def emit_y_router_half(s_i, hf, y_toks_half):
            yfm_ready = []
            for k, tt in enumerate(range(hf * 4, hf * 4 + 4)):
                b = tt % 2
                p.op("scalar", lambda e, tt=tt, b=b: e.copy(xbf[:, b, :], res3[:, tt, :]),
                     waits=[y_toks_half[k], st["xbf_free"][b], st["sq_last"]], inc="s")
                tok_ybf = p.tok("s")

                def ytr(e, b=b):
                    for blk in range(DB):
                        ins = e.transpose(pT[:, blk * 128:(blk + 1) * 128],
                                          xbf[:, b, blk * 128:(blk + 1) * 128], idb[:])
                    return ins
                p.op("tensor", ytr, waits=[tok_ybf, st["pT_free"], st["yfm_free"]], inc="t")
                tok_ytr = p.tok("t")
                st["xbf_free"][b] = tok_ytr
                p.op("scalar", lambda e, tt=tt: e.copy(
                    yfm[:, :, tt * 128:(tt + 1) * 128],
                    pT[:].rearrange("p (b c) -> p b c", c=128)),
                    waits=[tok_ytr], inc="s")
                st["pT_free"] = p.tok("s")
                yfm_ready.append(st["pT_free"])
            st["yfm_half"][hf] = yfm_ready[-1]
            ltm_toks = []
            for k, tt in enumerate(range(hf * 4, hf * 4 + 4)):
                pi = pnext()
                def lgmm(e, tt=tt, pi=pi):
                    for db in range(DB):
                        ins = e.matmul(P[pi][:, 0:E],
                                       yfm[:, db, tt * 128:(tt + 1) * 128],
                                       rt_s[:, db, :],
                                       start=(db == 0), stop=(db == DB - 1))
                    return ins
                p.op("tensor", lgmm, waits=[yfm_ready[k], st["P_free"][pi]], inc="t")
                tok_lg = p.tok("t")
                p.op("scalar", lambda e, tt=tt, pi=pi: e.copy(
                    ltm[:, tt * E:(tt + 1) * E], P[pi][:, 0:E]),
                    waits=[tok_lg], inc="s")
                st["P_free"][pi] = p.tok("s")
                ltm_toks.append(st["P_free"][pi])
            NH = 4
            cs = slice(hf * 4 * E, (hf * 4 + 4) * E)
            ss = slice(hf * 4, hf * 4 + 4)
            lv = lambda tns: tns[:, cs].rearrange("p (t e) -> p t e", e=E)
            sv = lambda tns: tns[:, ss]
            p.op("vector", lambda e: e.reduce_max(sv(p1b), lv(ltm), axis=AX.X),
                 waits=[ltm_toks[-1]], inc="v")
            t1v = p.tok("v")
            p.op("vector", lambda e: e.tensor_tensor(
                lv(m2b), lv(ltm), sv(p1b).unsqueeze(2).broadcast_to([128, NH, E]),
                ALU.subtract), waits=[t1v], inc="v")
            tok_lsh = p.tok("v")
            p.op("scalar", lambda e: e.activation(probs[:, cs], m2b[:, cs], AF.Exp),
                 waits=[tok_lsh], inc="s")
            tok_pexp = p.tok("s")
            p.op("vector", lambda e: e.reduce_sum(sv(p2b), lv(probs), axis=AX.X),
                 waits=[tok_pexp], inc="v")
            t2v = p.tok("v")
            p.op("vector", lambda e: e.reciprocal(sv(d21), sv(p2b)), waits=[t2v], inc="v")
            t3v = p.tok("v")
            p.op("vector", lambda e: e.tensor_tensor(
                lv(probs), lv(probs), sv(d21).unsqueeze(2).broadcast_to([128, NH, E]),
                ALU.mult), waits=[t3v], inc="v")
            t4v = p.tok("v")
            p.op("vector", lambda e: e.reduce_max(sv(p1b), lv(probs), axis=AX.X),
                 waits=[t4v], inc="v")
            t5v = p.tok("v")
            p.op("vector", lambda e: e.tensor_tensor(
                lv(m1b), lv(probs), sv(p1b).unsqueeze(2).broadcast_to([128, NH, E]),
                ALU.is_equal), waits=[t5v], inc="v")
            t6v = p.tok("v")
            p.op("vector", lambda e: e.scalar_tensor_tensor(
                lv(m2b), lv(m1b), -1e30, lv(probs), ALU.mult, ALU.add),
                waits=[t6v], inc="v")
            t7v = p.tok("v")
            p.op("vector", lambda e: e.reduce_max(sv(p2b), lv(m2b), axis=AX.X),
                 waits=[t7v], inc="v")
            t8v = p.tok("v")
            p.op("vector", lambda e: e.tensor_tensor(sv(d21), sv(p2b), sv(p1b),
                                                     ALU.subtract), waits=[t8v], inc="v")
            tok_d21 = p.tok("v")
            p.op("scalar", lambda e: e.activation(texp[:, ss], d21[:, ss], AF.Exp),
                 waits=[tok_d21], inc="s")
            tok_texp = p.tok("s")
            p.op("vector", lambda e: e.tensor_scalar_add(sv(n1b), sv(texp), 1.0),
                 waits=[tok_texp], inc="v")
            t9v = p.tok("v")
            p.op("vector", lambda e: e.tensor_tensor(
                lv(m2b), lv(probs), sv(p2b).unsqueeze(2).broadcast_to([128, NH, E]),
                ALU.is_equal), waits=[t8v], inc="v")
            t10v = p.tok("v")
            p.op("vector", lambda e: e.reciprocal(sv(n1b), sv(n1b)), waits=[t9v], inc="v")
            t11v = p.tok("v")
            p.op("vector", lambda e: e.scalar_tensor_tensor(
                lv(m2b), lv(m1b), -1.0, lv(m2b), ALU.mult, ALU.add),
                waits=[t10v], inc="v")
            t12v = p.tok("v")
            p.op("vector", lambda e: e.tensor_tensor(sv(n2b), sv(texp), sv(n1b), ALU.mult),
                 waits=[t11v], inc="v")
            t13v = p.tok("v")
            p.op("vector", lambda e: e.tensor_relu(m2b[:, cs], m2b[:, cs]),
                 waits=[t12v], inc="v")
            t14v = p.tok("v")
            p.op("vector", lambda e: e.tensor_tensor(
                lv(wfull), lv(m1b), sv(n1b).unsqueeze(2).broadcast_to([128, NH, E]),
                ALU.mult), waits=[t11v], inc="v")
            t15v = p.tok("v")
            p.op("vector", lambda e: e.tensor_tensor(
                lv(m2b), lv(m2b), sv(n2b).unsqueeze(2).broadcast_to([128, NH, E]),
                ALU.mult), waits=[t13v, t14v], inc="v")
            t16v = p.tok("v")
            p.op("vector", lambda e: e.tensor_tensor(wfull[:, cs], wfull[:, cs],
                                                     m2b[:, cs], ALU.add),
                 waits=[t15v, t16v], inc="v")
            st["tok_w_half"][hf] = p.tok("v")

        def emit_ffn_block(s_i, e_i, fh, gA, gB, g_toks, u_toks, d_toks, ln2=False):
            par = s_i % 2
            pump(gB, 2)
            for j in range(8):
                fb = fh * 8 + j
                i8 = fb // 2
                col = (fb % 2) * 128
                bw = (e_i * 8 + i8) % 2
                nxt = i8 + 1
                if nxt < 8 and nxt not in g_toks:
                    emit_gu(e_i, nxt, g_toks, u_toks)
                pg = pnext()
                def gmm(e, bw=bw, col=col, pg=pg):
                    for nb in range(2):
                        for db in range(DB):
                            ins = e.matmul(P[pg][:, nb * 512:(nb + 1) * 512],
                                           wgq[:, bw, db, col:col + 128],
                                           yfm[:, db, nb * 512:(nb + 1) * 512],
                                           start=(db == 0), stop=(db == DB - 1))
                    return ins
                p.op("tensor", gmm,
                     waits=[g_toks[i8], st["yfm_half"][1], st["P_free"][pg]], inc="t")
                tok_g = p.tok("t")
                if fb % 2 == 1:
                    st["wg_free"][bw] = tok_g
                p.op("scalar", lambda e, pg=pg: e.activation(fg[:], P[pg][:], AF.Silu),
                     waits=[tok_g, st["fg_free"][0]], inc="s")
                tok_silu = p.tok("s")
                st["P_free"][pg] = tok_silu
                pu = pnext()
                def umm(e, bw=bw, col=col, pu=pu):
                    for nb in range(2):
                        for db in range(DB):
                            ins = e.matmul(P[pu][:, nb * 512:(nb + 1) * 512],
                                           wuq[:, bw, db, col:col + 128],
                                           yfm[:, db, nb * 512:(nb + 1) * 512],
                                           start=(db == 0), stop=(db == DB - 1))
                    return ins
                p.op("tensor", umm, waits=[u_toks[i8], st["P_free"][pu]], inc="t")
                tok_u = p.tok("t")
                if fb % 2 == 1:
                    st["wu_free"][bw] = tok_u
                p.op("vector", lambda e, j=j, pu=pu, par=par: e.tensor_tensor(
                    bufP[par][:, j, :], fg[:], P[pu][:], ALU.mult),
                    waits=[tok_silu, tok_u,
                           st["bufP_free"][par] if fb == 0 else ("t", 0)],
                    inc="v")
                tok_fuse = p.tok("v")
                st["fg_free"][0] = tok_fuse
                st["P_free"][pu] = tok_fuse
                pump(gB, 2)
            tok_fuse_all = p.tok("v")
            for tt in range(NT):
                pd = pnext()
                def dmm(e, tt=tt, fh=fh, pd=pd, e_i=e_i, par=par):
                    for nn in range(2):
                        for j in range(8):
                            q4 = fh * 2 + j // 4
                            bw = (e_i * 4 + q4) % 3
                            ins = e.matmul(P[pd][:, nn * 512:(nn + 1) * 512],
                                           bufP[par][:, j, tt * 128:(tt + 1) * 128],
                                           wdq[:, bw, j % 4, nn * 512:(nn + 1) * 512],
                                           start=(j == 0), stop=(j == 7))
                    return ins
                p.op("tensor", dmm,
                     waits=[tok_fuse_all, d_toks[fh * 2], d_toks[fh * 2 + 1],
                            st["P_free"][pd]], inc="t")
                tok_d = p.tok("t")
                if tt == NT - 1:
                    st["bufP_free"][par] = tok_d
                    for q4 in (fh * 2, fh * 2 + 1):
                        st["wd_free"][(e_i * 4 + q4) % 3] = tok_d
                    if e_i == E - 1:
                        st["yfm_free"] = p.tok("t")
                p.op("vector", lambda e, tt=tt, pd=pd, e_i=e_i: e.scalar_tensor_tensor(
                    res3[:, tt, :], P[pd][:], wfull[:, tt * E + e_i:tt * E + e_i + 1],
                    res3[:, tt, :], ALU.mult, ALU.add),
                    waits=[tok_d, st["tok_w_half"][tt // 4]], inc="v")
                st["dacc"][tt] = p.tok("v")
                st["P_free"][pd] = st["dacc"][tt]
                if ln2:
                    emit_ln_stats(tt, st["dacc"][tt])
                pump(gB, 2)

        def emit_gu(e_i, i8, g_toks, u_toks):
            bw = (e_i * 8 + i8) % 2
            p.op("gpsimd", lambda e, e_i=e_i, i8=i8, bw=bw: e.dma_start(
                out=wgq[:, bw, :, :],
                in_=wg_d[e_i].rearrange("(b p) f -> p b f", p=128)[:, :, i8 * 256:(i8 + 1) * 256]),
                waits=[st["wg_free"][bw]], inc=f"dg{bw}", amt=16)
            g_toks[i8] = p.tok(f"dg{bw}")
            p.op("gpsimd", lambda e, e_i=e_i, i8=i8, bw=bw: e.dma_start(
                out=wuq[:, bw, :, :],
                in_=wu_d[e_i].rearrange("(b p) f -> p b f", p=128)[:, :, i8 * 256:(i8 + 1) * 256]),
                waits=[st["wu_free"][bw]], inc=f"du{bw}", amt=16)
            u_toks[i8] = p.tok(f"du{bw}")

        def emit_wd(e_i, q4, d_toks):
            bw = (e_i * 4 + q4) % 3
            p.op("gpsimd", lambda e, e_i=e_i, q4=q4, bw=bw: e.dma_start(
                out=wdq[:, bw, :, :],
                in_=wd_d[e_i].rearrange("(b p) d -> p b d", p=128)[:, q4 * 4:(q4 + 1) * 4, :]),
                waits=[st["wd_free"][bw]], inc=f"dd{bw}", amt=16)
            d_toks[q4] = p.tok(f"dd{bw}")

        def attention_gens(s_i):
            gA = deque()
            for tt in range(NT):
                gA.append(gen_xcast(s_i, tt))
            gA.append(gen_lat(s_i))
            gB = deque()
            gB.append(gen_qkv(s_i, 0))
            for tt in range(NT):
                if tt + 1 < NT:
                    gB.append(gen_qkv(s_i, tt + 1))
                gB.append(gen_attvec(s_i, tt))
            return gA, gB

        st["dacc"] = [None] * NT
        gA, gB = attention_gens(0)
        pump(gA, 6)
        init_di = emit_winit()
        flush(gA)
        flush(gB)
        emit_wout_load()
        rows0 = slice(0, S)
        p.op("sync", lambda e, r=rows0: e.dma_start(
            out=res3[:], in_=x_d[r, :].rearrange("(tt p) d -> p tt d", p=128)),
            waits=[tok_bdnc], inc="dr", amt=16)
        st["tok_xload"] = p.tok("dr")
        resadd = emit_attnout_resadd(0, ln1_stats=True)
        for hf in range(2):
            y_half = emit_ln_finish(hf * 4, hf * 4 + 4)
            emit_y_router_half(0, hf, y_half)

        for s_i in range(nsub):
            st["dacc"] = [None] * NT
            if s_i + 1 < nsub:
                gA, gB = attention_gens(s_i + 1)
                flush(gA)
            else:
                gB = deque()
            for e_i in range(E):
                g_toks = {}
                u_toks = {}
                d_toks = {}
                emit_gu(e_i, 0, g_toks, u_toks)
                emit_gu(e_i, 1, g_toks, u_toks)
                for fh in range(2):
                    emit_wd(e_i, fh * 2, d_toks)
                    emit_wd(e_i, fh * 2 + 1, d_toks)
                    emit_ffn_block(s_i, e_i, fh, gA, gB, g_toks, u_toks, d_toks,
                                   ln2=(e_i == E - 1 and fh == 1))
            flush(gB)
            if s_i + 1 < nsub:
                emit_wout_load()
            out_toks = emit_ln_finish(0, NT)
            rows = slice(s_i * S, (s_i + 1) * S)
            p.op("sync", lambda e, r=rows: e.dma_start(
                out=out_d[r, :].rearrange("(tt p) d -> p tt d", p=128), in_=res3[:]),
                waits=[out_toks[-1]], inc="so", amt=16)
            st["res_free"] = p.tok("so")
            if s_i + 1 < nsub:
                rows2 = slice((s_i + 1) * S, (s_i + 2) * S)
                p.op("sync", lambda e, r=rows2: e.dma_start(
                    out=res3[:], in_=x_d[r, :].rearrange("(tt p) d -> p tt d", p=128)),
                    waits=[st["res_free"]], inc="dr", amt=16)
                st["tok_xload"] = p.tok("dr")
                resadd = emit_attnout_resadd(s_i + 1, ln1_stats=True)
                for hf in range(2):
                    y_half = emit_ln_finish(hf * 4, hf * 4 + 4)
                    emit_y_router_half(s_i + 1, hf, y_half)

        p.op("sync", lambda e: e.nop(), waits=[st["res_free"]])

        with nc.Block() as block:
            @block.sync
            def _(eng):
                replay(p, "sync", eng, sems)

            @block.gpsimd
            def _(eng):
                replay(p, "gpsimd", eng, sems)

            @block.tensor
            def _(eng):
                replay(p, "tensor", eng, sems)

            @block.vector
            def _(eng):
                replay(p, "vector", eng, sems)

            @block.scalar
            def _(eng):
                replay(p, "scalar", eng, sems)

    return nc


def make_inputs(inp, ntok_per_core, cores):
    x = np.ascontiguousarray(np.asarray(inp["L0_attn_x"], dtype=np.float32).reshape(-1, D))
    perm = np.empty(2 * D, dtype=np.int64)
    for h in range(H):
        for d in range(HD):
            perm[h * HD + d] = h * 128 + d
    for d in range(HD):
        for g in range(H):
            perm[D + d * H + g] = g * 128 + 64 + d
    wup = np.ascontiguousarray(np.asarray(inp["L0_attn_Wup"], dtype=np.float32)[:, perm])
    bup = np.ascontiguousarray(np.asarray(inp["L0_attn_bup"], dtype=np.float32)[perm])
    common = dict(
        wq=np.ascontiguousarray(np.asarray(inp["L0_attn_Wq"], dtype=np.float32)),
        bq=np.ascontiguousarray(np.asarray(inp["L0_attn_bq"], dtype=np.float32)),
        wdn=np.ascontiguousarray(np.asarray(inp["L0_attn_Wdown"], dtype=np.float32)),
        bdn=np.ascontiguousarray(np.asarray(inp["L0_attn_bdown"], dtype=np.float32)),
        wup=wup, bup=bup,
        wo=np.ascontiguousarray(np.asarray(inp["L0_attn_Wout"], dtype=np.float32)),
        bo=np.ascontiguousarray(np.asarray(inp["L0_attn_bout"], dtype=np.float32)),
        rt=np.ascontiguousarray(np.asarray(inp["L0_ffn_router_kernel_DE"], dtype=np.float32)),
        wg=np.ascontiguousarray(np.asarray(inp["L0_ffn_kernel_gating_EDF"], dtype=np.float32)),
        wu=np.ascontiguousarray(np.asarray(inp["L0_ffn_kernel_up_EDF"], dtype=np.float32)),
        wd=np.ascontiguousarray(np.asarray(inp["L0_ffn_kernel_down_EFD"], dtype=np.float32)),
        idm=np.eye(128, dtype=np.float32),
    )
    in_maps = []
    for c in range(cores):
        m = dict(common)
        m["x0"] = np.ascontiguousarray(x[c * ntok_per_core:(c + 1) * ntok_per_core])
        in_maps.append(m)
    return in_maps


def kernel(**inputs):
    in_maps = make_inputs(inputs, 4096, NCORE)
    nc = build_program(nsub=4)
    res = run_bass_kernel_spmd(nc, in_maps, core_ids=list(range(NCORE)))
    out = np.concatenate([r["out"] for r in res.results], axis=0)
    return out.reshape(2, 16384, 1024)


# version 15
# speedup vs baseline: 1.1356x; 1.1064x over previous
import numpy as np
import ml_dtypes
import concourse.bass as bass
import concourse.mybir as mybir
from concourse.bass_utils import run_bass_kernel_spmd
from contextlib import ExitStack
from collections import deque

F32 = mybir.dt.float32
BF16 = mybir.dt.bfloat16
AF = mybir.ActivationFunctionType
ALU = mybir.AluOpType
AX = mybir.AxisListType

NCORE = 8
S = 1024
NT = S // 128
D = 1024
DB = D // 128
LATD = 256
LB = LATD // 128
FF = 2048
E = 8
H = 16
HD = 64
EPS = 1e-5

ENGINES = ("sync", "gpsimd", "tensor", "vector", "scalar")


class Plan:
    def __init__(self):
        self.ops = {e: [] for e in ENGINES}
        self.cnt = {}

    def sem(self, name):
        self.cnt[name] = 0

    def op(self, eng, fn, waits=(), inc=None, amt=1):
        ws = tuple((s, v) for (s, v) in waits if v > 0)
        self.ops[eng].append((ws, fn, inc, amt))
        if inc is not None:
            self.cnt[inc] += amt
            return (inc, self.cnt[inc])
        return None

    def tok(self, name):
        return (name, self.cnt[name])


def replay(plan, name, eng, sems):
    waited = {}
    for ws, fn, inc, amt in plan.ops[name]:
        for s, v in ws:
            if waited.get(s, 0) >= v:
                continue
            eng.wait_ge(sems[s], v)
            waited[s] = v
        ins = fn(eng)
        if inc is not None:
            ins.then_inc(sems[inc], amt)


def build_program(nsub):
    ntok = nsub * S
    nc = bass.Bass("TRN2", target_bir_lowering=False)
    x_d = nc.dram_tensor("x0", [ntok, D], F32, kind="ExternalInput").ap()
    wq_d = nc.dram_tensor("wq", [D, D], F32, kind="ExternalInput").ap()
    bq_d = nc.dram_tensor("bq", [D], F32, kind="ExternalInput").ap()
    wdn_d = nc.dram_tensor("wdn", [D, LATD], F32, kind="ExternalInput").ap()
    bdn_d = nc.dram_tensor("bdn", [LATD], F32, kind="ExternalInput").ap()
    wup_d = nc.dram_tensor("wup", [LATD, 2 * D], F32, kind="ExternalInput").ap()
    bup_d = nc.dram_tensor("bup", [2 * D], F32, kind="ExternalInput").ap()
    wo_d = nc.dram_tensor("wo", [D, D], F32, kind="ExternalInput").ap()
    bo_d = nc.dram_tensor("bo", [D], F32, kind="ExternalInput").ap()
    rt_d = nc.dram_tensor("rt", [D, E], F32, kind="ExternalInput").ap()
    wg_d = nc.dram_tensor("wg", [E, D, FF], F32, kind="ExternalInput").ap()
    wu_d = nc.dram_tensor("wu", [E, D, FF], F32, kind="ExternalInput").ap()
    wd_d = nc.dram_tensor("wd", [E, FF, D], F32, kind="ExternalInput").ap()
    id_d = nc.dram_tensor("idm", [128, 128], F32, kind="ExternalInput").ap()
    out_d = nc.dram_tensor("out", [ntok, D], F32, kind="ExternalOutput").ap()

    p = Plan()
    SEMS = ("di", "dj", "dk", "dr0", "dr1", "dc0", "dc1", "dg0", "dg1", "du0", "du1",
            "dd0", "dd1", "dd2", "dwo", "t", "v", "s", "so0", "so1")
    for sname in SEMS:
        p.sem(sname)

    with ExitStack() as ctx:
        en = ctx.enter_context
        wq_s = en(nc.sbuf_tensor("wq_s", [128, DB, D], BF16))
        wdn_s = en(nc.sbuf_tensor("wdn_s", [128, DB, LATD], BF16))
        wup_s = en(nc.sbuf_tensor("wup_s", [128, LB, 2 * D], BF16))
        rt_s = en(nc.sbuf_tensor("rt_s", [128, DB, E], BF16))
        bq_r = en(nc.sbuf_tensor("bq_r", [1, D], BF16))
        bup_r = en(nc.sbuf_tensor("bup_r", [1, 2 * D], BF16))
        bo_r = en(nc.sbuf_tensor("bo_r", [1, D], BF16))
        bdn_c = en(nc.sbuf_tensor("bdn_c", [128, LB], F32))
        idb = en(nc.sbuf_tensor("idb", [128, 128], BF16))
        idf8 = en(nc.sbuf_tensor("idf8", [8, 8], F32))
        ones_r = en(nc.sbuf_tensor("ones_r", [1, 128], BF16))
        res3 = en(nc.sbuf_tensor("res3", [128, NT, D], F32))
        bufP = [en(nc.sbuf_tensor(f"bufP{i}", [128, DB, S], BF16)) for i in range(2)]
        yfm = en(nc.sbuf_tensor("yfm", [128, DB, S], BF16))
        ofm = en(nc.sbuf_tensor("ofm", [128, DB, S], BF16))
        latfm = en(nc.sbuf_tensor("latfm", [128, LB, S], BF16))
        wdq = en(nc.sbuf_tensor("wdq", [128, 3, 4, D], BF16))
        wgq = en(nc.sbuf_tensor("wgq", [128, 2, DB, 256], BF16))
        wuq = en(nc.sbuf_tensor("wuq", [128, 2, DB, 256], BF16))
        qtm = en(nc.sbuf_tensor("qtm", [128, 2, D], BF16))
        kvtm = en(nc.sbuf_tensor("kvtm", [128, 2, 2 * D], BF16))
        tmp3 = en(nc.sbuf_tensor("tmp3", [128, 3, D], BF16))
        xbf = en(nc.sbuf_tensor("xbf", [128, 2, D], BF16))
        otm = en(nc.sbuf_tensor("otm", [128, D], BF16))
        fg = en(nc.sbuf_tensor("fg", [128, D], BF16))
        att = en(nc.sbuf_tensor("att", [128, H * H], F32))
        attp = en(nc.sbuf_tensor("attp", [128, H * H], BF16))
        amax = en(nc.sbuf_tensor("amax", [128, H], F32))
        asum = en(nc.sbuf_tensor("asum", [128, H], F32))
        arec = en(nc.sbuf_tensor("arec", [128, H], F32))
        ltm = en(nc.sbuf_tensor("ltm", [128, NT * E], F32))
        probs = en(nc.sbuf_tensor("probs", [128, NT * E], F32))
        m1b = en(nc.sbuf_tensor("m1b", [128, NT * E], F32))
        m2b = en(nc.sbuf_tensor("m2b", [128, NT * E], F32))
        wfull = en(nc.sbuf_tensor("wfull", [128, NT * E], F32))
        p1b = en(nc.sbuf_tensor("p1b", [128, NT], F32))
        p2b = en(nc.sbuf_tensor("p2b", [128, NT], F32))
        d21 = en(nc.sbuf_tensor("d21", [128, NT], F32))
        texp = en(nc.sbuf_tensor("texp", [128, NT], F32))
        n1b = en(nc.sbuf_tensor("n1b", [128, NT], F32))
        n2b = en(nc.sbuf_tensor("n2b", [128, NT], F32))
        ssum = en(nc.sbuf_tensor("ssum", [128, NT], F32))
        ssq = en(nc.sbuf_tensor("ssq", [128, NT], F32))
        mall = en(nc.sbuf_tensor("mall", [128, NT], F32))
        msq = en(nc.sbuf_tensor("msq", [128, NT], F32))
        varb = en(nc.sbuf_tensor("varb", [128, NT], F32))
        rstd = en(nc.sbuf_tensor("rstd", [128, NT], F32))
        P = [en(nc.psum_tensor(f"P{i}", [128, D], F32)) for i in range(3)]
        pT = en(nc.psum_tensor("pT", [128, D], BF16))
        pT2 = en(nc.psum_tensor("pT2", [128, D], BF16))

        sems = {s_: en(nc.semaphore(s_)) for s_ in SEMS}

        woA = wdq[:, 0, :, :]
        woB = wdq[:, 1, :, :]

        st = dict(
            P_free=[("s", 0)] * 3, rotP=0,
            pT_free=("s", 0), pT2_free=("s", 0),
            xbf_free=[("t", 0), ("t", 0)],
            qtm_free=[("v", 0), ("v", 0)], kv_free=[("v", 0), ("v", 0)],
            otm_free=("t", 0),
            fg_free=[("v", 0), ("v", 0)],
            yfm_half=[("s", 0), ("s", 0)], tok_w_half=[("v", 0), ("v", 0)],
            wg_free=[("t", 0), ("t", 0)], wu_free=[("t", 0), ("t", 0)],
            wd_free=[("t", 0), ("t", 0), ("t", 0)],
            bufP_free=[("t", 0), ("t", 0)],
            yfm_free=("t", 0), ofm_free=("t", 0), lat_free=("t", 0),
            res_free=[("so0", 0), ("so1", 0)], wo_loaded=("dwo", 0), wo_free=("t", 0),
            tok_xload=[("dr0", 0), ("dr1", 0)],
            prev_exp=("s", 0), tmp_free=("v", 0), sq_last=("s", 0), sum_last=("v", 0),
        )

        def pnext():
            i = st["rotP"] % 3
            st["rotP"] += 1
            return i

        p.op("sync", lambda e: e.dma_start(out=res3[0:LB, 0, 0:128], in_=bdn_d.rearrange("(c p) -> c p", p=128)), inc="dj", amt=16)
        p.op("sync", lambda e: e.dma_start(out=idf8[:], in_=id_d[0:8, 0:8]), inc="dj", amt=16)
        init_dj = p.tok("dj")
        p.op("gpsimd", lambda e: e.dma_start(out=idb[:], in_=id_d), inc="dk", amt=16)
        init_dk = p.tok("dk")

        def emit_winit():
            for dst, src_ in (
                (wq_s[:], wq_d.rearrange("(b p) n -> p b n", p=128)),
                (wdn_s[:], wdn_d.rearrange("(b p) n -> p b n", p=128)),
                (wup_s[:], wup_d.rearrange("(b p) n -> p b n", p=128)),
                (rt_s[:], rt_d.rearrange("(b p) n -> p b n", p=128)),
                (bq_r[:], bq_d[None, :]),
                (bup_r[:], bup_d[None, :]),
                (bo_r[:], bo_d[None, :]),
            ):
                p.op("gpsimd", lambda e, d=dst, s_=src_: e.dma_start(out=d, in_=s_),
                     inc="di", amt=16)
            return p.tok("di")
        p.op("vector", lambda e: e.memset(ones_r[:], 1.0), inc="v")
        tok_ones = p.tok("v")
        p.op("tensor", lambda e: e.transpose(P[0][:, 0:LB], res3[0:LB, 0, 0:128], idf8[0:LB, 0:LB]),
             waits=[init_dj], inc="t")
        p.op("scalar", lambda e: e.copy(bdn_c[:], P[0][:, 0:LB]), waits=[p.tok("t")], inc="s")
        tok_bdnc = p.tok("s")
        st["P_free"][0] = tok_bdnc

        def gen_xcast(s_i, tt):
            par = s_i % 2
            b = tt % 2
            r0 = s_i * S + tt * 128
            p.op("gpsimd", lambda e, r=r0, b=b: e.dma_start(
                out=xbf[:, b, :], in_=x_d[r:r + 128, :]),
                waits=[st["xbf_free"][b]], inc=f"dc{b}", amt=16)
            tok_cast = p.tok(f"dc{b}")
            yield

            def xtr(e, b=b):
                for blk in range(DB):
                    ins = e.transpose(pT[:, blk * 128:(blk + 1) * 128],
                                      xbf[:, b, blk * 128:(blk + 1) * 128], idb[:])
                return ins
            p.op("tensor", xtr, waits=[tok_cast, st["pT_free"], init_dk,
                                       st["bufP_free"][par]], inc="t")
            tok_xtr = p.tok("t")
            st["xbf_free"][b] = tok_xtr
            yield
            p.op("scalar", lambda e, tt=tt, par=par: e.copy(
                bufP[par][:, :, tt * 128:(tt + 1) * 128],
                pT[:].rearrange("p (b c) -> p b c", c=128)),
                waits=[tok_xtr], inc="s")
            st["pT_free"] = p.tok("s")
            st[f"xfm_ready_{tt}"] = st["pT_free"]
            yield

        def gen_lat(s_i):
            par = s_i % 2
            for lb in range(LB):
                pi = pnext()
                def latmm(e, lb=lb, pi=pi, par=par):
                    for nb in range(2):
                        for db in range(DB):
                            ins = e.matmul(P[pi][:, nb * 512:(nb + 1) * 512],
                                           wdn_s[:, db, lb * 128:(lb + 1) * 128],
                                           bufP[par][:, db, nb * 512:(nb + 1) * 512],
                                           start=(db == 0), stop=(db == DB - 1))
                    return ins
                p.op("tensor", latmm,
                     waits=[st["xfm_ready_7"], st["P_free"][pi], st["lat_free"], tok_bdnc,
                            init_di],
                     inc="t")
                tok_lat = p.tok("t")
                yield
                p.op("scalar", lambda e, lb=lb, pi=pi: e.activation(
                    latfm[:, lb, :], P[pi][:], AF.Identity, bias=bdn_c[:, lb:lb + 1]),
                    waits=[tok_lat], inc="s")
                st["P_free"][pi] = p.tok("s")
                st["lat_ready"] = p.tok("s")
                yield

        def gen_qkv(s_i, tt):
            par = s_i % 2
            pi = pnext()
            def qmm(e, tt=tt, pi=pi, par=par):
                for nb in range(2):
                    for db in range(DB):
                        e.matmul(P[pi][:, nb * 512:(nb + 1) * 512],
                                 bufP[par][:, db, tt * 128:(tt + 1) * 128],
                                 wq_s[:, db, nb * 512:(nb + 1) * 512],
                                 start=(db == 0), stop=False)
                    ins = e.matmul(P[pi][:, nb * 512:(nb + 1) * 512],
                                   ones_r[0:1, 0:128],
                                   bq_r[0:1, nb * 512:(nb + 1) * 512],
                                   start=False, stop=True)
                return ins
            p.op("tensor", qmm,
                 waits=[st[f"xfm_ready_{tt}"], st["P_free"][pi], init_di, tok_ones],
                 inc="t")
            tok_q = p.tok("t")
            yield
            b = tt % 2
            p.op("scalar", lambda e, pi=pi, b=b: e.activation(
                qtm[:, b, :], P[pi][:], AF.Identity, scale=np.float32(0.125)),
                waits=[tok_q, st["qtm_free"][b]], inc="s")
            st[f"tok_qe_{tt}"] = p.tok("s")
            st["P_free"][pi] = st[f"tok_qe_{tt}"]
            yield
            for half in range(2):
                pi = pnext()
                def kvmm(e, tt=tt, half=half, pi=pi):
                    for nb in range(2):
                        c0 = half * D + nb * 512
                        for lb in range(LB):
                            e.matmul(P[pi][:, nb * 512:(nb + 1) * 512],
                                     latfm[:, lb, tt * 128:(tt + 1) * 128],
                                     wup_s[:, lb, c0:c0 + 512],
                                     start=(lb == 0), stop=False)
                        ins = e.matmul(P[pi][:, nb * 512:(nb + 1) * 512],
                                       ones_r[0:1, 0:128],
                                       bup_r[0:1, c0:c0 + 512],
                                       start=False, stop=True)
                    return ins
                p.op("tensor", kvmm, waits=[st["lat_ready"], st["P_free"][pi]], inc="t")
                tok_kv = p.tok("t")
                yield
                p.op("scalar", lambda e, half=half, pi=pi, b=b: e.copy(
                    kvtm[:, b, half * D:(half + 1) * D], P[pi][:]),
                    waits=[tok_kv, st["kv_free"][b]], inc="s")
                st["P_free"][pi] = p.tok("s")
                yield
            st[f"tok_kve_{tt}"] = p.tok("s")

        def gen_attvec(s_i, tt):
            b = tt % 2
            kview = kvtm[:, b, 0:D].rearrange("p (g d) -> p g d", d=HD)
            vview = kvtm[:, b, D:2 * D].rearrange("p (d g) -> p d g", g=H)
            tok_qe = st[f"tok_qe_{tt}"]
            tok_kve = st[f"tok_kve_{tt}"]
            tok_m = [None] * H
            tok_r = [None] * H

            def qkm(h):
                def f(e, h=h, b=b, kview=kview):
                    return e.tensor_tensor(
                        tmp3[:, h % 3, :].rearrange("p (g d) -> p g d", d=HD),
                        qtm[:, b, h * HD:(h + 1) * HD].unsqueeze(1).broadcast_to([128, H, HD]),
                        kview, ALU.mult)
                return f

            def qkr(h):
                def f(e, h=h):
                    return e.reduce_sum(
                        att[:, h * H:(h + 1) * H],
                        tmp3[:, h % 3, :].rearrange("p (g d) -> p g d", d=HD), axis=AX.X)
                return f

            def emit_m(h):
                w = [tok_qe, tok_kve, st["prev_exp"], st["tmp_free"]] if h == 0 else (
                    [tok_r[h - 3]] if h >= 3 else [])
                p.op("vector", qkm(h), waits=w, inc="v")
                tok_m[h] = p.tok("v")

            def emit_r(h):
                p.op("vector", qkr(h), waits=[tok_m[h]], inc="v")
                tok_r[h] = p.tok("v")

            emit_m(0)
            yield
            emit_m(1)
            yield
            for h in range(2, H):
                emit_m(h)
                emit_r(h - 2)
                yield
            emit_r(H - 2)
            emit_r(H - 1)
            yield
            p.op("vector", lambda e: e.reduce_max(
                amax[:], att[:].rearrange("p (h g) -> p h g", g=H), axis=AX.X),
                waits=[tok_r[H - 1]], inc="v")
            tok_rmax = p.tok("v")
            p.op("vector", lambda e: e.tensor_tensor(
                att[:].rearrange("p (h g) -> p h g", g=H),
                att[:].rearrange("p (h g) -> p h g", g=H),
                amax[:].unsqueeze(2).broadcast_to([128, H, H]), ALU.subtract),
                waits=[tok_rmax], inc="v")
            tok_attm = p.tok("v")
            st["qtm_free"][b] = tok_attm
            yield
            p.op("scalar", lambda e: e.activation(attp[:], att[:], AF.Exp),
                 waits=[tok_attm], inc="s")
            tok_exp = p.tok("s")
            st["prev_exp"] = tok_exp
            yield
            p.op("vector", lambda e: e.reduce_sum(
                asum[:], attp[:].rearrange("p (h g) -> p h g", g=H), axis=AX.X),
                waits=[tok_exp], inc="v")
            tok_esum = p.tok("v")
            p.op("vector", lambda e: e.reciprocal(arec[:], asum[:]),
                 waits=[tok_esum], inc="v")
            tok_arec = p.tok("v")
            p.op("vector", lambda e: e.tensor_tensor(
                attp[:].rearrange("p (h g) -> p h g", g=H),
                attp[:].rearrange("p (h g) -> p h g", g=H),
                arec[:].unsqueeze(2).broadcast_to([128, H, H]), ALU.mult),
                waits=[tok_arec], inc="v")
            tok_probs = p.tok("v")
            yield

            tok_am = [None] * H
            tok_ar = [None] * H

            def avm(h):
                def f(e, h=h, vview=vview):
                    return e.tensor_tensor(
                        tmp3[:, h % 3, :].rearrange("p (d g) -> p d g", g=H),
                        attp[:, h * H:(h + 1) * H].unsqueeze(1).broadcast_to([128, HD, H]),
                        vview, ALU.mult)
                return f

            def avr(h):
                def f(e, h=h):
                    with e.bass.allow_low_precision(reason="bf16 matmul operand"):
                        return e.reduce_sum(
                            otm[:, h * HD:(h + 1) * HD],
                            tmp3[:, h % 3, :].rearrange("p (d g) -> p d g", g=H), axis=AX.X)
                return f

            def emit_am(h):
                w = [tok_probs, st["otm_free"]] if h == 0 else (
                    [tok_ar[h - 3]] if h >= 3 else [])
                p.op("vector", avm(h), waits=w, inc="v")
                tok_am[h] = p.tok("v")

            def emit_ar(h):
                p.op("vector", avr(h), waits=[tok_am[h]], inc="v")
                tok_ar[h] = p.tok("v")

            emit_am(0)
            yield
            emit_am(1)
            yield
            for h in range(2, H):
                emit_am(h)
                emit_ar(h - 2)
                yield
            emit_ar(H - 2)
            emit_ar(H - 1)
            tok_o = tok_ar[H - 1]
            st["kv_free"][b] = tok_o
            st["tmp_free"] = tok_o
            yield
            yield
            yield
            def otr(e):
                for blk in range(DB):
                    ins = e.transpose(pT2[:, blk * 128:(blk + 1) * 128],
                                      otm[:, blk * 128:(blk + 1) * 128], idb[:])
                return ins
            p.op("tensor", otr, waits=[tok_o, st["pT2_free"],
                                       st["ofm_free"] if tt == 0 else ("t", 0)],
                 inc="t")
            tok_otr = p.tok("t")
            st["otm_free"] = tok_otr
            yield
            p.op("scalar", lambda e, tt=tt: e.copy(
                ofm[:, :, tt * 128:(tt + 1) * 128],
                pT2[:].rearrange("p (b c) -> p b c", c=128)),
                waits=[tok_otr], inc="s")
            st["pT2_free"] = p.tok("s")
            st[f"ofm_ready_{tt}"] = st["pT2_free"]
            yield

        def pump(gens, n):
            k = 0
            while gens and k < n:
                try:
                    next(gens[0])
                    k += 1
                except StopIteration:
                    gens.popleft()

        def flush(gens):
            while gens:
                try:
                    next(gens[0])
                except StopIteration:
                    gens.popleft()

        def emit_ln_stats(tt, wait_tok):
            def sq(e, tt=tt):
                with e.bass.allow_low_precision(reason="scratch write; accum is fp32"):
                    return e.activation(xbf[:, 0, :], res3[:, tt, :], AF.Square,
                                        accum_out=ssq[:, tt:tt + 1])
            p.op("scalar", sq, waits=[wait_tok, st["sq_last"], st["xbf_free"][0]], inc="s")
            st["sq_last"] = p.tok("s")
            p.op("vector", lambda e, tt=tt: e.reduce_sum(
                ssum[:, tt:tt + 1], res3[:, tt, :], axis=AX.X),
                waits=[wait_tok], inc="v")
            st["sum_last"] = p.tok("v")

        def emit_ln_finish(lo, hi):
            sq_last = st["sq_last"]
            sl = slice(lo, hi)
            p.op("vector", lambda e: e.tensor_scalar_mul(mall[:, sl], ssum[:, sl], 1.0 / D),
                 waits=[st["sum_last"]], inc="v")
            tok_mall = p.tok("v")
            p.op("vector", lambda e: e.tensor_tensor(msq[:, sl], mall[:, sl], mall[:, sl], ALU.mult),
                 waits=[tok_mall], inc="v")
            tok_msq = p.tok("v")
            p.op("vector", lambda e: e.scalar_tensor_tensor(
                varb[:, sl], ssq[:, sl], 1.0 / D, msq[:, sl], ALU.mult, ALU.subtract),
                waits=[sq_last, tok_msq], inc="v")
            tok_var = p.tok("v")
            p.op("scalar", lambda e: e.activation(rstd[:, sl], varb[:, sl], AF.Ln,
                                                  bias=np.float32(EPS)),
                 waits=[tok_var], inc="s")
            tok_ln = p.tok("s")
            p.op("scalar", lambda e: e.activation(rstd[:, sl], rstd[:, sl], AF.Exp,
                                                  scale=np.float32(-0.5)),
                 waits=[tok_ln], inc="s")
            tok_rstd = p.tok("s")
            apply_toks = []
            for tt in range(lo, hi):
                p.op("vector", lambda e, tt=tt: e.tensor_scalar(
                    res3[:, tt, :], res3[:, tt, :], mall[:, tt:tt + 1],
                    rstd[:, tt:tt + 1], ALU.subtract, ALU.mult),
                    waits=[tok_rstd], inc="v")
                apply_toks.append(p.tok("v"))
            return apply_toks

        def emit_wout_load():
            p.op("gpsimd", lambda e: e.dma_start(
                out=wdq[:, 0:2, :, :].rearrange("p a b n -> p (a b) n"),
                in_=wo_d.rearrange("(b p) n -> p b n", p=128)),
                waits=[st["wd_free"][0], st["wd_free"][1]], inc="dwo", amt=16)
            st["wo_loaded"] = p.tok("dwo")

        def emit_attnout_resadd(s_i, ln1_stats=False):
            resadd_toks = []
            for tt in range(NT):
                pi = pnext()
                def aomm(e, tt=tt, pi=pi):
                    for nb in range(2):
                        for db in range(DB):
                            w_src = woA if db < 4 else woB
                            e.matmul(P[pi][:, nb * 512:(nb + 1) * 512],
                                     ofm[:, db, tt * 128:(tt + 1) * 128],
                                     w_src[:, db % 4, nb * 512:(nb + 1) * 512],
                                     start=(db == 0), stop=False)
                        ins = e.matmul(P[pi][:, nb * 512:(nb + 1) * 512],
                                       ones_r[0:1, 0:128],
                                       bo_r[0:1, nb * 512:(nb + 1) * 512],
                                       start=False, stop=True)
                    return ins
                p.op("tensor", aomm,
                     waits=[st[f"ofm_ready_{tt}"], st["P_free"][pi], st["wo_loaded"]],
                     inc="t")
                tok_ao = p.tok("t")
                if tt == NT - 1:
                    st["ofm_free"] = tok_ao
                    st["wo_free"] = tok_ao
                    st["wd_free"][0] = tok_ao
                    st["wd_free"][1] = tok_ao
                p.op("vector", lambda e, tt=tt, pi=pi: e.tensor_tensor(
                    res3[:, tt, :], res3[:, tt, :], P[pi][:], ALU.add),
                    waits=[tok_ao, st["tok_xload"][tt // 4]], inc="v")
                tok_resa = p.tok("v")
                st["P_free"][pi] = tok_resa
                resadd_toks.append(tok_resa)
                if ln1_stats:
                    emit_ln_stats(tt, tok_resa)
            return resadd_toks

        def emit_y_router_half(s_i, hf, y_toks_half):
            yfm_ready = []
            for k, tt in enumerate(range(hf * 4, hf * 4 + 4)):
                b = tt % 2
                p.op("scalar", lambda e, tt=tt, b=b: e.copy(xbf[:, b, :], res3[:, tt, :]),
                     waits=[y_toks_half[k], st["xbf_free"][b], st["sq_last"]], inc="s")
                tok_ybf = p.tok("s")

                def ytr(e, b=b):
                    for blk in range(DB):
                        ins = e.transpose(pT[:, blk * 128:(blk + 1) * 128],
                                          xbf[:, b, blk * 128:(blk + 1) * 128], idb[:])
                    return ins
                p.op("tensor", ytr, waits=[tok_ybf, st["pT_free"], st["yfm_free"]], inc="t")
                tok_ytr = p.tok("t")
                st["xbf_free"][b] = tok_ytr
                p.op("scalar", lambda e, tt=tt: e.copy(
                    yfm[:, :, tt * 128:(tt + 1) * 128],
                    pT[:].rearrange("p (b c) -> p b c", c=128)),
                    waits=[tok_ytr], inc="s")
                st["pT_free"] = p.tok("s")
                yfm_ready.append(st["pT_free"])
            st["yfm_half"][hf] = yfm_ready[-1]
            ltm_toks = []
            for k, tt in enumerate(range(hf * 4, hf * 4 + 4)):
                pi = pnext()
                def lgmm(e, tt=tt, pi=pi):
                    for db in range(DB):
                        ins = e.matmul(P[pi][:, 0:E],
                                       yfm[:, db, tt * 128:(tt + 1) * 128],
                                       rt_s[:, db, :],
                                       start=(db == 0), stop=(db == DB - 1))
                    return ins
                p.op("tensor", lgmm, waits=[yfm_ready[k], st["P_free"][pi]], inc="t")
                tok_lg = p.tok("t")
                p.op("scalar", lambda e, tt=tt, pi=pi: e.copy(
                    ltm[:, tt * E:(tt + 1) * E], P[pi][:, 0:E]),
                    waits=[tok_lg], inc="s")
                st["P_free"][pi] = p.tok("s")
                ltm_toks.append(st["P_free"][pi])
            NH = 4
            cs = slice(hf * 4 * E, (hf * 4 + 4) * E)
            ss = slice(hf * 4, hf * 4 + 4)
            lv = lambda tns: tns[:, cs].rearrange("p (t e) -> p t e", e=E)
            sv = lambda tns: tns[:, ss]
            p.op("vector", lambda e: e.reduce_max(sv(p1b), lv(ltm), axis=AX.X),
                 waits=[ltm_toks[-1]], inc="v")
            t1v = p.tok("v")
            p.op("vector", lambda e: e.tensor_tensor(
                lv(m2b), lv(ltm), sv(p1b).unsqueeze(2).broadcast_to([128, NH, E]),
                ALU.subtract), waits=[t1v], inc="v")
            tok_lsh = p.tok("v")
            p.op("scalar", lambda e: e.activation(probs[:, cs], m2b[:, cs], AF.Exp),
                 waits=[tok_lsh], inc="s")
            tok_pexp = p.tok("s")
            p.op("vector", lambda e: e.reduce_sum(sv(p2b), lv(probs), axis=AX.X),
                 waits=[tok_pexp], inc="v")
            t2v = p.tok("v")
            p.op("vector", lambda e: e.reciprocal(sv(d21), sv(p2b)), waits=[t2v], inc="v")
            t3v = p.tok("v")
            p.op("vector", lambda e: e.tensor_tensor(
                lv(probs), lv(probs), sv(d21).unsqueeze(2).broadcast_to([128, NH, E]),
                ALU.mult), waits=[t3v], inc="v")
            t4v = p.tok("v")
            p.op("vector", lambda e: e.reduce_max(sv(p1b), lv(probs), axis=AX.X),
                 waits=[t4v], inc="v")
            t5v = p.tok("v")
            p.op("vector", lambda e: e.tensor_tensor(
                lv(m1b), lv(probs), sv(p1b).unsqueeze(2).broadcast_to([128, NH, E]),
                ALU.is_equal), waits=[t5v], inc="v")
            t6v = p.tok("v")
            p.op("vector", lambda e: e.scalar_tensor_tensor(
                lv(m2b), lv(m1b), -1e30, lv(probs), ALU.mult, ALU.add),
                waits=[t6v], inc="v")
            t7v = p.tok("v")
            p.op("vector", lambda e: e.reduce_max(sv(p2b), lv(m2b), axis=AX.X),
                 waits=[t7v], inc="v")
            t8v = p.tok("v")
            p.op("vector", lambda e: e.tensor_tensor(sv(d21), sv(p2b), sv(p1b),
                                                     ALU.subtract), waits=[t8v], inc="v")
            tok_d21 = p.tok("v")
            p.op("scalar", lambda e: e.activation(texp[:, ss], d21[:, ss], AF.Exp),
                 waits=[tok_d21], inc="s")
            tok_texp = p.tok("s")
            p.op("vector", lambda e: e.tensor_scalar_add(sv(n1b), sv(texp), 1.0),
                 waits=[tok_texp], inc="v")
            t9v = p.tok("v")
            p.op("vector", lambda e: e.tensor_tensor(
                lv(m2b), lv(probs), sv(p2b).unsqueeze(2).broadcast_to([128, NH, E]),
                ALU.is_equal), waits=[t8v], inc="v")
            t10v = p.tok("v")
            p.op("vector", lambda e: e.reciprocal(sv(n1b), sv(n1b)), waits=[t9v], inc="v")
            t11v = p.tok("v")
            p.op("vector", lambda e: e.scalar_tensor_tensor(
                lv(m2b), lv(m1b), -1.0, lv(m2b), ALU.mult, ALU.add),
                waits=[t10v], inc="v")
            t12v = p.tok("v")
            p.op("vector", lambda e: e.tensor_tensor(sv(n2b), sv(texp), sv(n1b), ALU.mult),
                 waits=[t11v], inc="v")
            t13v = p.tok("v")
            p.op("vector", lambda e: e.tensor_relu(m2b[:, cs], m2b[:, cs]),
                 waits=[t12v], inc="v")
            t14v = p.tok("v")
            p.op("vector", lambda e: e.tensor_tensor(
                lv(wfull), lv(m1b), sv(n1b).unsqueeze(2).broadcast_to([128, NH, E]),
                ALU.mult), waits=[t11v], inc="v")
            t15v = p.tok("v")
            p.op("vector", lambda e: e.tensor_tensor(
                lv(m2b), lv(m2b), sv(n2b).unsqueeze(2).broadcast_to([128, NH, E]),
                ALU.mult), waits=[t13v, t14v], inc="v")
            t16v = p.tok("v")
            p.op("vector", lambda e: e.tensor_tensor(wfull[:, cs], wfull[:, cs],
                                                     m2b[:, cs], ALU.add),
                 waits=[t15v, t16v], inc="v")
            st["tok_w_half"][hf] = p.tok("v")

        def emit_ffn_block(s_i, e_i, fh, gA, gB, g_toks, u_toks, d_toks, ln2=False):
            par = s_i % 2
            pump(gB, 2)
            for j in range(8):
                fb = fh * 8 + j
                i8 = fb // 2
                col = (fb % 2) * 128
                bw = (e_i * 8 + i8) % 2
                nxt = i8 + 1
                if nxt < 8 and nxt not in g_toks:
                    emit_gu(e_i, nxt, g_toks, u_toks)
                pg = pnext()
                def gmm(e, bw=bw, col=col, pg=pg):
                    for nb in range(2):
                        for db in range(DB):
                            ins = e.matmul(P[pg][:, nb * 512:(nb + 1) * 512],
                                           wgq[:, bw, db, col:col + 128],
                                           yfm[:, db, nb * 512:(nb + 1) * 512],
                                           start=(db == 0), stop=(db == DB - 1))
                    return ins
                p.op("tensor", gmm,
                     waits=[g_toks[i8], st["yfm_half"][1], st["P_free"][pg]], inc="t")
                tok_g = p.tok("t")
                if fb % 2 == 1:
                    st["wg_free"][bw] = tok_g
                p.op("scalar", lambda e, pg=pg: e.activation(fg[:], P[pg][:], AF.Silu),
                     waits=[tok_g, st["fg_free"][0]], inc="s")
                tok_silu = p.tok("s")
                st["P_free"][pg] = tok_silu
                pu = pnext()
                def umm(e, bw=bw, col=col, pu=pu):
                    for nb in range(2):
                        for db in range(DB):
                            ins = e.matmul(P[pu][:, nb * 512:(nb + 1) * 512],
                                           wuq[:, bw, db, col:col + 128],
                                           yfm[:, db, nb * 512:(nb + 1) * 512],
                                           start=(db == 0), stop=(db == DB - 1))
                    return ins
                p.op("tensor", umm, waits=[u_toks[i8], st["P_free"][pu]], inc="t")
                tok_u = p.tok("t")
                if fb % 2 == 1:
                    st["wu_free"][bw] = tok_u
                p.op("vector", lambda e, j=j, pu=pu, par=par: e.tensor_tensor(
                    bufP[par][:, j, :], fg[:], P[pu][:], ALU.mult),
                    waits=[tok_silu, tok_u,
                           st["bufP_free"][par] if fb == 0 else ("t", 0)],
                    inc="v")
                tok_fuse = p.tok("v")
                st["fg_free"][0] = tok_fuse
                st["P_free"][pu] = tok_fuse
                pump(gB, 2)
            tok_fuse_all = p.tok("v")
            for tt in range(NT):
                pd = pnext()
                def dmm(e, tt=tt, fh=fh, pd=pd, e_i=e_i, par=par):
                    for nn in range(2):
                        for j in range(8):
                            q4 = fh * 2 + j // 4
                            bw = (e_i * 4 + q4) % 3
                            ins = e.matmul(P[pd][:, nn * 512:(nn + 1) * 512],
                                           bufP[par][:, j, tt * 128:(tt + 1) * 128],
                                           wdq[:, bw, j % 4, nn * 512:(nn + 1) * 512],
                                           start=(j == 0), stop=(j == 7))
                    return ins
                p.op("tensor", dmm,
                     waits=[tok_fuse_all, d_toks[fh * 2], d_toks[fh * 2 + 1],
                            st["P_free"][pd]], inc="t")
                tok_d = p.tok("t")
                if tt == NT - 1:
                    st["bufP_free"][par] = tok_d
                    for q4 in (fh * 2, fh * 2 + 1):
                        st["wd_free"][(e_i * 4 + q4) % 3] = tok_d
                    if e_i == E - 1:
                        st["yfm_free"] = p.tok("t")
                p.op("vector", lambda e, tt=tt, pd=pd, e_i=e_i: e.scalar_tensor_tensor(
                    res3[:, tt, :], P[pd][:], wfull[:, tt * E + e_i:tt * E + e_i + 1],
                    res3[:, tt, :], ALU.mult, ALU.add),
                    waits=[tok_d, st["tok_w_half"][tt // 4]], inc="v")
                st["dacc"][tt] = p.tok("v")
                st["P_free"][pd] = st["dacc"][tt]
                if ln2:
                    emit_ln_stats(tt, st["dacc"][tt])
                pump(gB, 2)

        def emit_gu(e_i, i8, g_toks, u_toks):
            bw = (e_i * 8 + i8) % 2
            p.op("gpsimd", lambda e, e_i=e_i, i8=i8, bw=bw: e.dma_start(
                out=wgq[:, bw, :, :],
                in_=wg_d[e_i].rearrange("(b p) f -> p b f", p=128)[:, :, i8 * 256:(i8 + 1) * 256]),
                waits=[st["wg_free"][bw]], inc=f"dg{bw}", amt=16)
            g_toks[i8] = p.tok(f"dg{bw}")
            p.op("gpsimd", lambda e, e_i=e_i, i8=i8, bw=bw: e.dma_start(
                out=wuq[:, bw, :, :],
                in_=wu_d[e_i].rearrange("(b p) f -> p b f", p=128)[:, :, i8 * 256:(i8 + 1) * 256]),
                waits=[st["wu_free"][bw]], inc=f"du{bw}", amt=16)
            u_toks[i8] = p.tok(f"du{bw}")

        def emit_wd(e_i, q4, d_toks):
            bw = (e_i * 4 + q4) % 3
            p.op("gpsimd", lambda e, e_i=e_i, q4=q4, bw=bw: e.dma_start(
                out=wdq[:, bw, :, :],
                in_=wd_d[e_i].rearrange("(b p) d -> p b d", p=128)[:, q4 * 4:(q4 + 1) * 4, :]),
                waits=[st["wd_free"][bw]], inc=f"dd{bw}", amt=16)
            d_toks[q4] = p.tok(f"dd{bw}")

        def attention_gens(s_i):
            gA = deque()
            for tt in range(NT):
                gA.append(gen_xcast(s_i, tt))
            gA.append(gen_lat(s_i))
            gB = deque()
            gB.append(gen_qkv(s_i, 0))
            for tt in range(NT):
                if tt + 1 < NT:
                    gB.append(gen_qkv(s_i, tt + 1))
                gB.append(gen_attvec(s_i, tt))
            return gA, gB

        st["dacc"] = [None] * NT
        gA, gB = attention_gens(0)
        pump(gA, 6)
        init_di = emit_winit()
        flush(gA)
        flush(gB)
        emit_wout_load()
        for hf in range(2):
            r0 = hf * 512
            p.op("sync", lambda e, r0=r0, hf=hf: e.dma_start(
                out=res3[:, hf * 4:(hf + 1) * 4, :],
                in_=x_d[r0:r0 + 512, :].rearrange("(tt p) d -> p tt d", p=128)),
                waits=[tok_bdnc], inc=f"dr{hf}", amt=16)
            st["tok_xload"][hf] = p.tok(f"dr{hf}")
        resadd = emit_attnout_resadd(0, ln1_stats=True)
        for hf in range(2):
            y_half = emit_ln_finish(hf * 4, hf * 4 + 4)
            emit_y_router_half(0, hf, y_half)

        for s_i in range(nsub):
            st["dacc"] = [None] * NT
            if s_i + 1 < nsub:
                gA, gB = attention_gens(s_i + 1)
                flush(gA)
            else:
                gB = deque()
            for e_i in range(E):
                g_toks = {}
                u_toks = {}
                d_toks = {}
                emit_gu(e_i, 0, g_toks, u_toks)
                emit_gu(e_i, 1, g_toks, u_toks)
                for fh in range(2):
                    emit_wd(e_i, fh * 2, d_toks)
                    emit_wd(e_i, fh * 2 + 1, d_toks)
                    emit_ffn_block(s_i, e_i, fh, gA, gB, g_toks, u_toks, d_toks,
                                   ln2=(e_i == E - 1 and fh == 1))
            flush(gB)
            if s_i + 1 < nsub:
                emit_wout_load()
            out_toks = emit_ln_finish(0, NT)
            for hf in range(2):
                r0 = s_i * S + hf * 512
                p.op("sync", lambda e, r0=r0, hf=hf: e.dma_start(
                    out=out_d[r0:r0 + 512, :].rearrange("(tt p) d -> p tt d", p=128),
                    in_=res3[:, hf * 4:(hf + 1) * 4, :]),
                    waits=[out_toks[hf * 4 + 3]], inc=f"so{hf}", amt=16)
                st["res_free"][hf] = p.tok(f"so{hf}")
                if s_i + 1 < nsub:
                    r1 = (s_i + 1) * S + hf * 512
                    p.op("sync", lambda e, r1=r1, hf=hf: e.dma_start(
                        out=res3[:, hf * 4:(hf + 1) * 4, :],
                        in_=x_d[r1:r1 + 512, :].rearrange("(tt p) d -> p tt d", p=128)),
                        waits=[st["res_free"][hf]], inc=f"dr{hf}", amt=16)
                    st["tok_xload"][hf] = p.tok(f"dr{hf}")
            if s_i + 1 < nsub:
                resadd = emit_attnout_resadd(s_i + 1, ln1_stats=True)
                for hf in range(2):
                    y_half = emit_ln_finish(hf * 4, hf * 4 + 4)
                    emit_y_router_half(s_i + 1, hf, y_half)

        p.op("sync", lambda e: e.nop(), waits=[st["res_free"][0], st["res_free"][1]])

        with nc.Block() as block:
            @block.sync
            def _(eng):
                replay(p, "sync", eng, sems)

            @block.gpsimd
            def _(eng):
                replay(p, "gpsimd", eng, sems)

            @block.tensor
            def _(eng):
                replay(p, "tensor", eng, sems)

            @block.vector
            def _(eng):
                replay(p, "vector", eng, sems)

            @block.scalar
            def _(eng):
                replay(p, "scalar", eng, sems)

    return nc


def make_inputs(inp, ntok_per_core, cores):
    x = np.ascontiguousarray(np.asarray(inp["L0_attn_x"], dtype=np.float32).reshape(-1, D))
    perm = np.empty(2 * D, dtype=np.int64)
    for h in range(H):
        for d in range(HD):
            perm[h * HD + d] = h * 128 + d
    for d in range(HD):
        for g in range(H):
            perm[D + d * H + g] = g * 128 + 64 + d
    wup = np.ascontiguousarray(np.asarray(inp["L0_attn_Wup"], dtype=np.float32)[:, perm])
    bup = np.ascontiguousarray(np.asarray(inp["L0_attn_bup"], dtype=np.float32)[perm])
    common = dict(
        wq=np.ascontiguousarray(np.asarray(inp["L0_attn_Wq"], dtype=np.float32)),
        bq=np.ascontiguousarray(np.asarray(inp["L0_attn_bq"], dtype=np.float32)),
        wdn=np.ascontiguousarray(np.asarray(inp["L0_attn_Wdown"], dtype=np.float32)),
        bdn=np.ascontiguousarray(np.asarray(inp["L0_attn_bdown"], dtype=np.float32)),
        wup=wup, bup=bup,
        wo=np.ascontiguousarray(np.asarray(inp["L0_attn_Wout"], dtype=np.float32)),
        bo=np.ascontiguousarray(np.asarray(inp["L0_attn_bout"], dtype=np.float32)),
        rt=np.ascontiguousarray(np.asarray(inp["L0_ffn_router_kernel_DE"], dtype=np.float32)),
        wg=np.ascontiguousarray(np.asarray(inp["L0_ffn_kernel_gating_EDF"], dtype=np.float32)),
        wu=np.ascontiguousarray(np.asarray(inp["L0_ffn_kernel_up_EDF"], dtype=np.float32)),
        wd=np.ascontiguousarray(np.asarray(inp["L0_ffn_kernel_down_EFD"], dtype=np.float32)),
        idm=np.eye(128, dtype=np.float32),
    )
    in_maps = []
    for c in range(cores):
        m = dict(common)
        m["x0"] = np.ascontiguousarray(x[c * ntok_per_core:(c + 1) * ntok_per_core])
        in_maps.append(m)
    return in_maps


def kernel(**inputs):
    in_maps = make_inputs(inputs, 4096, NCORE)
    nc = build_program(nsub=4)
    res = run_bass_kernel_spmd(nc, in_maps, core_ids=list(range(NCORE)))
    out = np.concatenate([r["out"] for r in res.results], axis=0)
    return out.reshape(2, 16384, 1024)
